# version 07
# baseline (speedup 1.0000x reference)
import numpy as np
import ml_dtypes
from contextlib import ExitStack
import concourse.bass as bass
import concourse.mybir as mybir
from concourse.bass_utils import run_bass_kernel_spmd

F32 = mybir.dt.float32
F16 = mybir.dt.float16
AX = mybir.AxisListType.X
AF = mybir.ActivationFunctionType
OP = mybir.AluOpType

NCORES = 8
T = 4096
D = 1024
NT = T // 128
A_HEADS = 8
B_HEADS = 16
LN2_EPS = 1e-6
LN_EPS = 1e-5
N_EXP = 8

DEBUG = {"phase": "all"}

ENGINES = ("pe", "act", "dve", "sp", "gp")


class Sem:
    def __init__(self, h):
        self.h = h
        self.n = 0
        self.eng = None

    def inc(self, k=1):
        self.n += k
        return self.n


class Plan:
    def __init__(self, nc, es):
        self.nc = nc
        self.es = es
        self.q = {e: [] for e in ENGINES}

    def sem(self, name):
        return Sem(self.es.enter_context(self.nc.semaphore(name)))

    def emit(self, eng, f):
        self.q[eng].append(f)

    def wait(self, eng, sem, val=None):
        v = sem.n if val is None else val
        if v > 0:
            self.emit(eng, lambda e, h=sem.h, vv=v: e.wait_ge(h, vv))

    def op(self, eng, fn, sem=None, k=1):
        if sem is None:
            self.emit(eng, fn)
            return None
        v = sem.inc(k)
        self.emit(eng, lambda e, f=fn, h=sem.h, kk=k: f(e).then_inc(h, kk))
        return v

    def dma(self, eng, sem, out_ap, in_ap):
        assert sem.eng in (None, eng), "sem shared across queues"
        sem.eng = eng
        if sem.n > 0:
            self.wait(eng, sem)
        return self.op(eng, lambda e, o=out_ap, i=in_ap: e.dma_start(out=o, in_=i), sem, 16)

    def wait_all_dma(self, eng, sems):
        for sm in sems:
            self.wait(eng, sm)


def ceil128(n):
    return ((n + 127) // 128) * 128


def build_routing(router_indices, routing_weights):
    ri = np.asarray(router_indices)
    rw = np.asarray(routing_weights)
    ntok, topk = ri.shape
    per_e = [[] for _ in range(N_EXP)]
    for t in range(ntok):
        for j in range(topk):
            per_e[int(ri[t, j])].append((t, j))
    caps = [ceil128(len(sl)) for sl in per_e]
    offs = np.cumsum([0] + caps)
    G = int(offs[-1])
    GT = max(G // 128, 1)
    g_pos = np.zeros(max(G, 1), dtype=np.int64)
    g_w = np.zeros(max(G, 1), dtype=np.float32)
    g_real = np.zeros(max(G, 1), dtype=bool)
    segs = []
    for e in range(N_EXP):
        sl = sorted(per_e[e], key=lambda tj: (tj[0] // 128, tj[0], tj[1]))
        g0e = int(offs[e])
        for i, (t, j) in enumerate(sl):
            g = g0e + i
            g_pos[g] = t
            g_w[g] = rw[t, int(ri[t, j])]
            g_real[g] = True
        i = 0
        n = len(sl)
        while i < n:
            tau = sl[i][0] // 128
            i1 = i
            while i1 < n and sl[i1][0] // 128 == tau:
                i1 += 1
            segs.append([e, tau, g0e + i, g0e + i1])
            i = i1
        if caps[e] > n and n > 0:
            segs[-1][3] = g0e + caps[e]
    PT = np.zeros((128, max(G, 1)), dtype=np.float16)
    for e, tau, ga, gb in segs:
        for g in range(ga, gb):
            if g_real[g]:
                PT[int(g_pos[g]) % 128, g] = 1
    windows = []
    for e in range(N_EXP):
        a, b = int(offs[e]), int(offs[e]) + caps[e]
        w0 = a
        while w0 < b:
            w1 = min(w0 + 512, b)
            wsegs = [(tau, max(ga, w0), min(gb, w1))
                     for (ee, tau, ga, gb) in segs if ee == e and gb > w0 and ga < w1]
            windows.append((e, w0, w1, wsegs))
            w0 = w1
    pair_map = {}
    for g in range(G):
        if not g_real[g]:
            continue
        key = (g // 128, int(g_pos[g]) // 128)
        if key not in pair_map:
            pair_map[key] = np.zeros((128, 128), dtype=np.float32)
        pair_map[key][g % 128, int(g_pos[g]) % 128] += 1.0
    scat_keys = sorted(pair_map.keys(), key=lambda k: (k[1], k[0]))
    if scat_keys:
        scat_blocks = np.stack([pair_map[k] for k in scat_keys], axis=0).astype(np.float16)
    else:
        scat_blocks = np.zeros((1, 128, 128), dtype=np.float16)
    wcol = np.zeros((128, GT), dtype=np.float32)
    for g in range(G):
        wcol[g % 128, g // 128] = g_w[g]
    assert G <= 5120, G
    return dict(G=G, GT=GT, caps=caps, offs=offs, segs=segs, windows=windows,
                PT=np.ascontiguousarray(PT), scat_keys=scat_keys,
                scat_blocks=np.ascontiguousarray(scat_blocks), wcol=wcol)


def build_program(rt):
    nc = bass.Bass("TRN2", target_bir_lowering=False)
    G, GT = rt["G"], rt["GT"]
    NBLK = len(rt["scat_keys"])

    din = lambda n, sh, dt=F32: nc.dram_tensor(n, sh, dt, kind="ExternalInput").ap()
    x_in = din("x", [T, D])
    Wq = din("Wq", [2048, D])
    Wkv = din("Wkv", [64, D])
    Wku = din("Wku", [2048, 64])
    Wvu = din("Wvu", [2048, 64])
    Wo = din("Wo", [D, 2048])
    qw = din("qw", [D, D]); kw = din("kw", [D, D]); vw = din("vw", [D, D]); ow = din("ow", [D, D])
    qbdup = din("qbdup", [128, 16]); kbdup = din("kbdup", [128, 16])
    vb_rep = din("vb_rep", [128, D]); ob_rep = din("ob_rep", [128, D])
    gam_rep = din("gam_rep", [128, D]); bet_rep = din("bet_rep", [128, D])
    proj4 = din("proj4", [128, 256])
    W1p = din("W1p", [N_EXP, D, 2048])
    W2 = din("W2", [N_EXP, D, D])
    b1cols = din("b1cols", [128, 16 * N_EXP])
    b2rep = din("b2rep", [N_EXP, 128, D])
    identity = din("identity", [128, 128], F16)
    trimask = din("trimask", [128, 128], F16)
    PTg = din("PTg", [128, max(G, 1)], F16)
    scat = din("scat", [max(NBLK, 1) * 128, 128], F16)
    wcol_in = din("wcol", [128, GT])
    out = nc.dram_tensor("out", [T, D], F32, kind="ExternalOutput").ap()

    dsc = lambda n, sh, dt=F32: nc.dram_tensor(n, sh, dt, kind="Internal").ap()
    xres = dsc("xres", [T, D])
    attnTa = dsc("attnTa", [16 * 128, T], F16)
    attnTb = dsc("attnTb", [B_HEADS * 64, T], F16)
    down_sc = dsc("down_sc", [GT * 128, D], F16)
    W1c = dsc("W1c", [N_EXP * 1024, 2048], F16)
    W2c = dsc("W2c", [N_EXP * 1024, 1024], F16)
    dbg = nc.dram_tensor("dbg", [6 * 128, T], F16, kind="Internal").ap()

    es = ExitStack()
    with es:
        A1 = es.enter_context(nc.sbuf_tensor("A1", [128, 32768], F16))
        A2 = es.enter_context(nc.sbuf_tensor("A2", [128, 24576], F16))
        A3 = es.enter_context(nc.sbuf_tensor("A3", [128, 40960], F16))
        id_sb = es.enter_context(nc.sbuf_tensor("id_sb", [128, 128], F16))
        tri_sb = es.enter_context(nc.sbuf_tensor("tri_sb", [128, 128], F16))
        projT_aug = es.enter_context(nc.sbuf_tensor("projT_aug", [128, 256], F16))
        qb8 = es.enter_context(nc.sbuf_tensor("qb8", [128, 16], F32))
        kb8 = es.enter_context(nc.sbuf_tensor("kb8", [128, 16], F32))
        colT = es.enter_context(nc.sbuf_tensor("colT", [128, 16], F32))
        pb = [es.enter_context(nc.psum_tensor(f"pb{i}", [128, 512], F32)) for i in range(6)]
        pt = [es.enter_context(nc.psum_tensor(f"pt{i}", [128, 512], F16)) for i in range(2)]

        def v1(ar, c0, cols):
            return ar.ap()[:, c0:c0 + cols]

        def vf(ar, c0, cols_f32):
            return ar.ap().bitcast(F32)[:, c0 // 2:c0 // 2 + cols_f32]

        col = {n: colT.ap()[:, i:i + 1] for i, n in enumerate(
            ("sum", "mean", "var", "rstd", "rcp", "s2", "m2", "v2", "r2", "rc2"))}

        XT = [v1(A1, i * 4096, 4096) for i in range(8)]
        xn_ffn = [v1(A1, t * 1024, 1024) for t in range(16)]
        ITT = [v1(A1, 16384 + j * 512, 512) for j in range(8)]
        gupA = vf(A1, 20480, 512)
        gupB = vf(A1, 21504, 512)
        gupC = vf(A1, 22528, 512)
        dstage = [v1(A1, 23552 + d * 1024, 1024) for d in range(4)]
        b2r = vf(A1, 27648, 1024)
        sblk = [v1(A1, 29696 + i * 128, 128) for i in range(2)]
        b1c = vf(A1, 29952, 128)
        wcs = vf(A1, 30208, GT)
        ptw = [v1(A1, 30336 + i * 512, 512) for i in range(2)]
        WqT = [v1(A2, r * 2048, 2048) for r in range(8)]
        kuT = [A2.ap()[0:64, 16384 + c * 128:16384 + (c + 1) * 128] for c in range(16)]
        vu_h = [A2.ap()[0:64, 18432 + h * 256:18432 + (h + 1) * 256] for h in range(8)]
        wkvT = [v1(A2, 20480 + c * 64, 64) for c in range(8)]
        woT = [v1(A2, c * 1024, 1024) for c in range(16)]
        qwB = [v1(A2, c * 1024, 1024) for c in range(8)]
        kwB = [v1(A2, 8192 + c * 1024, 1024) for c in range(8)]
        vwbuf = [v1(A2, 16384 + i * 512, 512) for i in range(2)]
        qdup = v1(A2, 17408, 1024)
        kdup = v1(A2, 18432, 1024)
        qdupB = v1(A2, 21504, 1024)
        kdupB = v1(A2, 22528, 1024)
        vbr_b = vf(A2, 19456, 1024)
        owB = [v1(A2, c * 1024, 1024) for c in range(8)]
        W1J = [v1(A2, sl * 2048, 2048) for sl in range(2)]
        W2B2 = [[v1(A2, 4096 + sl * 8192 + k * 1024, 1024) for k in range(8)] for sl in range(2)]
        qT = [v1(A3, mc * 4096, 4096) for mc in range(2)]
        kT = [v1(A3, 8192 + mc * 4096, 4096) for mc in range(2)]
        vA = v1(A3, 16384, 8224)
        aoT = [v1(A3, 24608 + mc * 4096, 4096) for mc in range(2)]
        latT = A3.ap()[0:64, 32800:36896]
        expS = [v1(A3, 36896 + i * 512, 512) for i in range(2)]
        attn_nat = v1(A3, 37920, 256)
        QTa = v1(A3, 0, 4096)
        KTa = v1(A3, 4096, 4096)
        qpT = [v1(A3, 8192 + mc * 4096, 4096) for mc in range(2)]
        kpT = [v1(A3, 16384 + mc * 4096, 4096) for mc in range(2)]
        kpN = v1(A3, 24576, 8192)
        vB = v1(A3, 32768, 2080)
        atT = A3.ap()[0:64, 34848:38944]
        kvsb = [[v1(A3, 38944 + (p * 2 + i) * 65, 65) for i in range(2)] for p in range(2)]
        atm = [v1(A3, 39204 + i * 128, 128) for i in range(2)]
        kvf32 = [vf(A3, 39464 + i * 264, 65) for i in range(2)]
        battn = v1(A3, 39996, 64)
        stage3 = [v1(A3, mc * 2048, 2048) for mc in range(2)]
        xt3 = [vf(A3, 8192 + i * 2048, 1024) for i in range(2)]
        bstage = [v1(A3, i * 1024, 1024) for i in range(2)]
        xn2f = [vf(A3, 2048 + i * 2048, 1024) for i in range(2)]
        residf = [vf(A3, 22528 + i * 2048, 1024) for i in range(2)]
        obrf = vf(A3, 6144, 1024)
        gmrf = vf(A3, 8192, 1024)
        btrf = vf(A3, 10240, 1024)
        XgT = [v1(A3, k * max(G, 1), max(G, 1)) for k in range(8)]

        def ln_scratch(base):
            xs = [vf(A3, base + i * 2048, 1024) for i in range(2)]
            cs = [vf(A3, base + 4096 + i * 2048, 1024) for i in range(2)]
            xb = [v1(A3, base + 8192 + i * 1024, 1024) for i in range(2)]
            return xs, cs, xb

        P = Plan(nc, es)
        sPE = P.sem("sPE")
        sACT = P.sem("sACT")
        sDVE = P.sem("sDVE")
        cC = P.sem("cC")
        cX = [P.sem(f"cX{i}") for i in range(2)]
        cSG = [P.sem(f"cSG{i}") for i in range(2)]
        cST = [P.sem(f"cST{i}") for i in range(2)]
        cOUT = [P.sem(f"cOUT{i}") for i in range(2)]
        cPTW = [P.sem(f"cPTW{i}") for i in range(2)]
        cSB = [P.sem(f"cSB{i}") for i in range(2)]
        cDST = [P.sem(f"cDST{i}") for i in range(2)]
        cDS = [P.sem(f"cDS{i}") for i in range(4)]
        cB2 = P.sem("cB2")
        cTA = [P.sem(f"cTA{i}") for i in range(2)]
        cTB = P.sem("cTB")
        cXR = P.sem("cXR")
        gW = [P.sem(f"gW{i}") for i in range(4)]
        cW1 = [P.sem(f"cW1{i}") for i in range(2)]
        cW2 = [P.sem(f"cW2{i}") for i in range(2)]
        gV = [P.sem(f"gV{i}") for i in range(2)]
        SP_SEMS = [cC] + cX + cSG + cST + cOUT + cPTW + cSB + cDST + cDS + [cB2] + cTA + [cTB, cXR] + cW1 + cW2
        GP_SEMS = gW + gV

        pt_last = [0, 0]
        ln_state = {0: 0, 1: 0}
        cent_cur = [None, None]

        def transpose_blk(src_slice, dst_slice, bank, width=128, p_in=128):
            ptv = pt[bank].ap()[0:width, 0:p_in]
            P.wait("pe", sACT, pt_last[bank])
            P.op("pe", lambda e, s=src_slice, p=ptv, ii=id_sb.ap()[0:p_in, 0:p_in]: e.transpose(p, s, ii), sPE)
            pev = sPE.n
            P.wait("act", sPE, pev)
            P.op("act", lambda e, p=ptv, d=dst_slice: e.copy(d, p), sACT)
            pt_last[bank] = sACT.n
            return sACT.n

        def ln_tile(src_f32, sq_out_bf, out_bf=None, out_f32=None, eps=LN_EPS, slot=0,
                    gamma_beta=None):
            cs, cm, cv, cr, crcp = ((col["sum"], col["mean"], col["var"], col["rstd"], col["rcp"])
                                    if slot == 0 else
                                    (col["s2"], col["m2"], col["v2"], col["r2"], col["rc2"]))
            ce = cent_cur[slot]
            P.wait("dve", sACT, ln_state[slot])
            P.wait("dve", sDVE)
            P.op("dve", lambda e, s=src_f32, c=cs: e.reduce_sum(c, s, AX), sDVE)
            P.wait("dve", sDVE)
            P.op("dve", lambda e, c=cs, m=cm: e.tensor_scalar_mul(m, c, 1.0 / D), sDVE)
            P.wait("dve", sDVE)
            P.op("dve", lambda e, s=src_f32, m=cm, c=ce: e.tensor_scalar(
                c, s, m, None, OP.subtract), sDVE)
            dve_cent = sDVE.n
            P.wait("act", sDVE, dve_cent)
            P.op("act", lambda e, c=ce, v=cv, q=sq_out_bf: e.activation(
                q, c, AF.Square, accum_out=v), sACT)
            ln_state[slot] = sACT.n
            P.wait("dve", sACT, ln_state[slot])
            P.op("dve", lambda e, v=cv: e.tensor_scalar(
                v, v, 1.0 / D, eps, OP.mult, OP.add), sDVE)
            P.wait("dve", sDVE)
            P.op("dve", lambda e, v=cv, r=crcp: e.reciprocal(r, v), sDVE)
            dve_rcp = sDVE.n
            P.wait("act", sDVE, dve_rcp)
            P.op("act", lambda e, r=crcp, o=cr: e.activation(o, r, AF.Sqrt), sACT)
            act_rstd = sACT.n
            P.wait("dve", sACT, act_rstd)
            if gamma_beta is None:
                if out_bf is not None:
                    P.op("dve", lambda e, c=ce, r=cr, o=out_bf: e.tensor_scalar(
                        o, c, r, None, OP.mult), sDVE)
                if out_f32 is not None:
                    P.op("dve", lambda e, c=ce, r=cr, o=out_f32: e.tensor_scalar(
                        o, c, r, None, OP.mult), sDVE)
            else:
                g_rep, b_rep = gamma_beta
                P.op("dve", lambda e, c=ce, r=cr, g=g_rep, o=out_f32: e.scalar_tensor_tensor(
                    o, c, r, g, OP.mult, OP.mult), sDVE)
                P.wait("dve", sDVE)
                P.op("dve", lambda e, b=b_rep, o=out_f32: e.tensor_tensor(
                    o, o, b, OP.add), sDVE)
            return sDVE.n

        for dst, src in ((id_sb.ap()[:, :], identity), (tri_sb.ap()[:, :], trimask),
                         (qb8.ap()[:, :], qbdup), (kb8.ap()[:, :], kbdup)):
            P.dma("sp", cC, dst, src)
        P.dma("sp", cXR, xres, x_in)
        pstage = vf(A3, 36896, 256)
        P.dma("sp", cC, pstage, proj4)
        P.wait("dve", cC)
        P.op("dve", lambda e: e.tensor_copy(projT_aug.ap()[:, :], pstage), sDVE)
        P.op("dve", lambda e: e.tensor_scalar_mul(qb8.ap()[:, :], qb8.ap()[:, :], 0.125), sDVE)
        P.op("dve", lambda e: e.tensor_scalar_mul(kb8.ap()[:, :], kb8.ap()[:, :], 0.125), sDVE)
        dve_p0 = sDVE.n

        wq_stage = [v1(A1, cix * 1024, 1024) for cix in range(16)]
        for cix in range(16):
            P.dma("gp", gW[cix % 4], wq_stage[cix], Wq[cix * 128:(cix + 1) * 128, :])
        P.wait_all_dma("pe", gW)
        P.wait("pe", cC)
        for r in range(8):
            for c in range(16):
                transpose_blk(wq_stage[c][:, r * 128:(r + 1) * 128],
                              WqT[r][:, c * 128:(c + 1) * 128], (r * 16 + c) % 2)
        kvst = v1(A1, 16384, 1024)
        kust = v1(A1, 17408, 1024)
        vust = v1(A1, 18432, 1024)
        P.dma("gp", gW[0], kvst[0:64, :], Wkv)
        P.dma("gp", gW[1], kust.rearrange("p (c w) -> p c w", c=16),
              Wku.rearrange("(c p) w -> c p w", p=128).transpose([1, 0, 2]))
        P.dma("gp", gW[2], vust.rearrange("p (c w) -> p c w", c=16),
              Wvu.rearrange("(c p) w -> c p w", p=128).transpose([1, 0, 2]))
        P.wait_all_dma("pe", gW[:3])
        for c in range(8):
            transpose_blk(kvst[0:64, c * 128:(c + 1) * 128], wkvT[c], c % 2, width=128, p_in=64)
        for c in range(16):
            transpose_blk(kust[:, c * 64:(c + 1) * 64], kuT[c], c % 2, width=64, p_in=128)
        for c in range(16):
            transpose_blk(vust[:, c * 64:(c + 1) * 64],
                          A2.ap()[0:64, 18432 + c * 128:18432 + (c + 1) * 128], c % 2,
                          width=64, p_in=128)
        pe_prelude = sPE.n

        for e_i in range(N_EXP):
            P.dma("gp", gW[e_i % 4], W1c[e_i * 1024:(e_i + 1) * 1024, :],
                  W1p[e_i, :, :])
            P.dma("gp", gW[e_i % 4], W2c[e_i * 1024:(e_i + 1) * 1024, :],
                  W2[e_i, :, :])
        gw_cast = [g.n for g in gW]

        xt_s, ce_s, xb_s = ln_scratch(16384)
        cent_cur[0], cent_cur[1] = ce_s[0], ce_s[1]
        dve_lnmark = {}
        for i in range(NT):
            sl = i % 2
            if i >= 2:
                P.wait("sp", sDVE, dve_lnmark[i - 2])
            if i == 0:
                P.wait("sp", sDVE, dve_p0)
            P.dma("sp", cX[sl], xt_s[sl], x_in[i * 128:(i + 1) * 128, :])
            P.wait("dve", cX[sl])
            ln_tile(xt_s[sl], xb_s[sl], out_bf=xb_s[sl], slot=sl)
            dve_lnmark[i] = sDVE.n
            P.wait("pe", sDVE, dve_lnmark[i])
            for c in range(8):
                transpose_blk(xb_s[sl][:, c * 128:(c + 1) * 128],
                              XT[c][:, i * 128:(i + 1) * 128], c % 2)
        act_p1 = sACT.n

        P.wait("pe", sACT, act_p1)
        lat_mark = {}
        for scnk in range(8):
            bank = pb[scnk % 2]
            if scnk >= 2:
                P.wait("pe", sACT, lat_mark[scnk - 2])
            for kc in range(8):
                P.op("pe", lambda e, b=bank, kc0=kc, s0=scnk, st=(kc == 0), sp=(kc == 7):
                     e.matmul(b.ap()[0:64, 0:512], wkvT[kc0],
                              XT[kc0][:, s0 * 512:(s0 + 1) * 512], start=st, stop=sp),
                     sPE if kc == 7 else None)
            pev = sPE.n
            P.wait("act", sPE, pev)
            P.op("act", lambda e, b=bank, s0=scnk: e.copy(latT[:, s0 * 512:(s0 + 1) * 512],
                                                          b.ap()[0:64, 0:512]), sACT)
            lat_mark[scnk] = sACT.n
        P.wait("pe", sACT)
        P.wait("dve", sDVE)
        P.op("dve", lambda e: e.memset(
            vA.rearrange("p (t w) -> p t w", w=257)[:, :, 256:257], 1.0), sDVE)
        dve_ones = sDVE.n
        P.wait("pe", sDVE, dve_ones)

        for h in range(A_HEADS):
            P.wait("act", sPE)
            P.wait_all_dma("act", cTA)
            qm = {}
            for it, (mc, scnk) in enumerate([(m, sc) for m in range(2) for sc in range(8)]):
                bank = pb[it % 2]
                if it >= 2:
                    P.wait("pe", sACT, qm[it - 2])
                for kc in range(8):
                    P.op("pe", lambda e, b=bank, kc0=kc, m0=mc, s0=scnk, h0=h,
                         st=(kc == 0), sp=(kc == 7):
                         e.matmul(b.ap()[:, 0:512],
                                  WqT[kc0][:, h0 * 256 + m0 * 128:h0 * 256 + (m0 + 1) * 128],
                                  XT[kc0][:, s0 * 512:(s0 + 1) * 512], start=st, stop=sp),
                         sPE if kc == 7 else None)
                pev = sPE.n
                P.wait("act", sPE, pev)
                P.op("act", lambda e, b=bank, m0=mc, s0=scnk:
                     e.copy(qT[m0][:, s0 * 512:(s0 + 1) * 512], b.ap()[:, 0:512]), sACT)
                qm[it] = sACT.n
            P.wait("pe", sACT)
            km = {}
            for it, (mc, scnk) in enumerate([(m, sc) for m in range(2) for sc in range(8)]):
                bank = pb[it % 2]
                if it >= 2:
                    P.wait("pe", sACT, km[it - 2])
                P.op("pe", lambda e, b=bank, m0=mc, s0=scnk, h0=h:
                     e.matmul(b.ap()[:, 0:512], kuT[h0 * 2 + m0],
                              latT[:, s0 * 512:(s0 + 1) * 512], start=True, stop=True), sPE)
                pev = sPE.n
                P.wait("act", sPE, pev)
                P.op("act", lambda e, b=bank, m0=mc, s0=scnk:
                     e.copy(kT[m0][:, s0 * 512:(s0 + 1) * 512], b.ap()[:, 0:512]), sACT)
                km[it] = sACT.n
            P.wait("pe", sACT)
            vm = {}
            for i in range(NT):
                bank = pb[i % 2]
                if i >= 2:
                    P.wait("pe", sACT, vm[i - 2])
                P.op("pe", lambda e, b=bank, i0=i, h0=h:
                     e.matmul(b.ap()[:, 0:256], latT[:, i0 * 128:(i0 + 1) * 128],
                              vu_h[h0], start=True, stop=True), sPE)
                pev = sPE.n
                P.wait("act", sPE, pev)
                P.op("act", lambda e, b=bank, i0=i:
                     e.copy(vA[:, i0 * 257:i0 * 257 + 256], b.ap()[:, 0:256]), sACT)
                vm[i] = sACT.n
            P.wait("pe", sACT)

            for sbk in range(8):
                def emit_S(kt):
                    sbank = pb[kt % 2]
                    for mc in range(2):
                        P.op("pe", lambda e, b=sbank, m0=mc, k0=kt, s0=sbk,
                             st=(mc == 0), sp=(mc == 1):
                             e.matmul(b.ap()[:, 0:512], kT[m0][:, k0 * 128:(k0 + 1) * 128],
                                      qT[m0][:, s0 * 512:(s0 + 1) * 512], start=st, stop=sp),
                             sPE if mc == 1 else None)
                    pev = sPE.n
                    P.wait("act", sPE, pev)
                    P.op("act", lambda e, b=sbank, k0=kt:
                         e.activation(expS[k0 % 2], b.ap()[:, 0:512], AF.Exp,
                                      scale=0.0625), sACT)
                    return sACT.n
                exp_mark = {}
                exp_mark[0] = emit_S(0)
                for kt in range(NT):
                    if kt + 1 < NT:
                        exp_mark[kt + 1] = emit_S(kt + 1)
                    P.wait("pe", sACT, exp_mark[kt])
                    for qt in range(4):
                        P.op("pe", lambda e, q0=qt, k0=kt:
                             e.matmul(pb[2 + q0].ap()[:, 0:257],
                                      expS[k0 % 2][:, q0 * 128:(q0 + 1) * 128],
                                      vA[:, k0 * 257:(k0 + 1) * 257],
                                      start=(k0 == 0), stop=(k0 == NT - 1)),
                             sPE if qt == 3 else None)
                pe_pv = sPE.n
                for qt in range(4):
                    if qt == 0:
                        P.wait("dve", sPE, pe_pv)
                    P.op("dve", lambda e, q0=qt: e.reciprocal(col["rcp"],
                         pb[2 + q0].ap()[:, 256:257]), sDVE)
                    P.wait("dve", sDVE)
                    P.op("dve", lambda e, q0=qt: e.tensor_scalar(
                        attn_nat, pb[2 + q0].ap()[:, 0:256], col["rcp"], None,
                        OP.mult), sDVE)
                    dve_attn = sDVE.n
                    P.wait("pe", sDVE, dve_attn)
                    for mc in range(2):
                        transpose_blk(attn_nat[:, mc * 128:(mc + 1) * 128],
                                      aoT[mc][:, sbk * 512 + qt * 128:sbk * 512 + (qt + 1) * 128],
                                      mc)
                    P.wait("dve", sPE)
            act_head = sACT.n
            P.wait("sp", sACT, act_head)
            for mc in range(2):
                P.dma("sp", cTA[mc], attnTa[(h * 2 + mc) * 128:(h * 2 + mc + 1) * 128, :], aoT[mc])

        P.wait("gp", sPE)
        wo_stage = [v1(A1, cix * 2048, 2048) for cix in range(8)]
        for cix in range(8):
            P.dma("gp", gW[cix % 4], wo_stage[cix], Wo[cix * 128:(cix + 1) * 128, :])
        P.wait_all_dma("pe", gW)
        P.wait("act", sPE)
        for c in range(16):
            for r in range(8):
                transpose_blk(wo_stage[r][:, c * 128:(c + 1) * 128],
                              woT[c][:, r * 128:(r + 1) * 128], (c * 8 + r) % 2)
        act_woT = sACT.n
        P.wait("pe", sACT, act_woT)
        P.wait_all_dma("sp", cTA)
        P.wait("sp", cXR)

        pe_t3 = {}
        st_t3 = {}
        dv_t3 = {}
        for i in range(NT):
            sl = i % 2
            if i >= 2:
                P.wait("sp", sPE, pe_t3[i - 2])
                P.wait("sp", cST[sl], st_t3[i - 2])
            P.dma("sp", cSG[sl], stage3[sl].rearrange("p (c w) -> p c w", c=16),
                  attnTa.rearrange("(c p) s -> c p s", p=128)[:, :, i * 128:(i + 1) * 128]
                  .transpose([1, 0, 2]))
            P.dma("sp", cX[sl], xt3[sl], xres[i * 128:(i + 1) * 128, :])
            P.wait("pe", cSG[sl])
            P.wait("pe", cX[sl])
            if i >= 2:
                P.wait("pe", sDVE, dv_t3[i - 2])
            for half in range(2):
                bank = pb[half + 2 * sl]
                for c in range(16):
                    P.op("pe", lambda e, b=bank, c0=c, h0=half, s0=sl,
                         st=(c == 0), sp=(c == 15):
                         e.matmul(b.ap()[:, 0:512], stage3[s0][:, c0 * 128:(c0 + 1) * 128],
                                  woT[c0][:, h0 * 512:(h0 + 1) * 512], start=st, stop=sp),
                         sPE if c == 15 else None)
            pev = sPE.n
            pe_t3[i] = pev
            P.wait("dve", sPE, pev)
            P.wait("dve", cX[sl])
            for half in range(2):
                P.op("dve", lambda e, h0=half, s0=sl: e.tensor_tensor(
                    xt3[s0][:, h0 * 512:(h0 + 1) * 512], pb[h0 + 2 * s0].ap()[:, 0:512],
                    xt3[s0][:, h0 * 512:(h0 + 1) * 512], OP.add), sDVE)
            dvev = sDVE.n
            dv_t3[i] = dvev
            P.wait("sp", sDVE, dvev)
            P.dma("sp", cST[sl], xres[i * 128:(i + 1) * 128, :], xt3[sl])
            st_t3[i] = cST[sl].n

        def write_out_from_xres():
            P.wait_all_dma("sp", SP_SEMS)
            for i in range(NT):
                P.dma("sp", cXR, out[i * 128:(i + 1) * 128, :], xres[i * 128:(i + 1) * 128, :])
            P.wait_all_dma("sp", SP_SEMS)
            P.wait_all_dma("gp", GP_SEMS)

        if DEBUG["phase"] == "attn_a":
            write_out_from_xres()
            final_build(P)
            return nc, es

        def ffn_phase(write_out_tiles):
            ds_ctr = [0]
            P.wait("sp", sPE)
            P.dma("sp", cB2, b1c, b1cols)
            P.dma("sp", cB2, wcs, wcol_in)
            xtF, ceF, xbF = ln_scratch(0)
            cent_cur[0], cent_cur[1] = ceF[0], ceF[1]
            P.wait("dve", sPE)
            P.wait("sp", sPE)
            P.wait("act", sPE)
            P.wait_all_dma("act", cST)
            P.wait_all_dma("dve", cST)
            P.wait_all_dma("act", cOUT)
            P.wait_all_dma("dve", cOUT)
            P.wait_all_dma("pe", cOUT)
            marks = {}
            for t in range(16):
                sl = t % 2
                if t >= 2:
                    P.wait("sp", sDVE, marks[t - 2])
                P.dma("sp", cX[sl], xtF[sl], xres[t * 128:(t + 1) * 128, :])
                P.wait("dve", cX[sl])
                ln_tile(xtF[sl], xbF[sl], out_bf=xn_ffn[t], slot=sl)
                marks[t] = sDVE.n
            dve_lnF = sDVE.n

            P.wait("pe", sDVE, dve_lnF)
            pe_w = {}
            for wi, (eexp, w0, w1, wsegs) in enumerate(rt["windows"]):
                wl = w1 - w0
                if wi >= 2:
                    P.wait("sp", sPE, pe_w[wi - 2])
                P.dma("sp", cPTW[wi % 2], ptw[wi % 2][:, 0:wl], PTg[:, w0:w1])
                P.wait("pe", cPTW[wi % 2])
                for k in range(8):
                    bank = pb[(wi * 8 + k) % 2]
                    P.wait("pe", sACT)
                    nseg = len(wsegs)
                    for si, (tau, ga, gb) in enumerate(wsegs):
                        P.op("pe", lambda e, b=bank, k0=k, t0=tau, a=ga, bb=gb, o=w0,
                             w2=(wi % 2), st=(si == 0), sp=(si == nseg - 1):
                             e.matmul(b.ap()[:, a - o:bb - o],
                                      xn_ffn[t0][:, k0 * 128:(k0 + 1) * 128],
                                      ptw[w2][:, a - o:bb - o], start=st, stop=sp),
                             sPE if si == nseg - 1 else None)
                    pev = sPE.n
                    P.wait("act", sPE, pev)
                    P.op("act", lambda e, b=bank, k0=k, a=w0, bb=w1:
                         e.copy(XgT[k0][:, a:bb], b.ap()[:, 0:bb - a]), sACT)
                pe_w[wi] = sPE.n
            act_gather = sACT.n
            P.wait("pe", sACT, act_gather)

            jobs = []
            for e_i in range(N_EXP):
                if rt["caps"][e_i] == 0:
                    continue
                for (e2, w0, w1, _segs) in [w for w in rt["windows"] if w[0] == e_i]:
                    for j in range(8):
                        jobs.append((e_i, w0, w1 - w0, j))
            jld = {}
            jpe = {}
            w2pe = {}

            def load_jslice(idx):
                e_i, _w0, _wl, j = jobs[idx]
                sl = idx % 2
                P.wait("sp", sPE, jpe.get(idx - 2, 0))
                P.dma("sp", cW1[sl], W1J[sl].rearrange("p (k w) -> p k w", k=8),
                      W1c.rearrange("(a p) w -> a p w", p=128)[e_i * 8:e_i * 8 + 8, :,
                                                              j * 256:(j + 1) * 256]
                      .transpose([1, 0, 2]))
                jld[idx] = cW1[sl]

            exps = [e for e in range(N_EXP) if rt["caps"][e] > 0]

            def load_w2(eidx):
                sl = eidx % 2
                P.wait("sp", sPE, w2pe.get(eidx - 2, 0))
                e_i2 = exps[eidx]
                P.dma("sp", cW2[sl],
                      A2.ap()[:, 4096 + sl * 8192:4096 + sl * 8192 + 8192]
                      .rearrange("p (k w) -> p k w", k=8),
                      W2c.rearrange("(a p) w -> a p w", p=128)[e_i2 * 8:e_i2 * 8 + 8, :, :]
                      .transpose([1, 0, 2]))

            P.wait("sp", sPE)
            P.wait("sp", sACT)
            P.wait("sp", sDVE)
            for gg, vv in zip(gW, gw_cast):
                P.wait("sp", gg, vv)
            load_jslice(0)
            load_jslice(1)
            load_w2(0)
            guw_state = {}
            idx = 0
            eidx = -1
            prev_e = None
            for e_i in range(N_EXP):
                cap = rt["caps"][e_i]
                if cap == 0:
                    continue
                eidx += 1
                if eidx + 1 < len(exps):
                    load_w2(eidx + 1)
                P.wait("sp", sDVE)
                P.dma("sp", cB2, b2r, b2rep[e_i, :, :])
                P.wait("dve", cB2)
                ew = [w for w in rt["windows"] if w[0] == e_i]
                guw = guw_state
                for (e2, w0, w1, _segs) in ew:
                    wl = w1 - w0
                    ngt = wl // 128
                    for j in range(8):
                        sl = idx % 2
                        P.wait("pe", jld[idx])
                        bg = pb[(j % 2) * 2]
                        bu = pb[(j % 2) * 2 + 1]
                        P.wait("pe", sDVE, guw.get((j % 2), 0))
                        for k in range(8):
                            P.op("pe", lambda e, b=bg, k0=k, s2=sl, a=w0, L=wl,
                                 st=(k == 0), sp=(k == 7):
                                 e.matmul(b.ap()[:, 0:L], W1J[s2][:, k0 * 256:k0 * 256 + 128],
                                          XgT[k0][:, a:a + L], start=st, stop=sp),
                                 sPE if k == 7 else None)
                        for k in range(8):
                            P.op("pe", lambda e, b=bu, k0=k, s2=sl, a=w0, L=wl,
                                 st=(k == 0), sp=(k == 7):
                                 e.matmul(b.ap()[:, 0:L],
                                          W1J[s2][:, k0 * 256 + 128:(k0 + 1) * 256],
                                          XgT[k0][:, a:a + L], start=st, stop=sp),
                                 sPE if k == 7 else None)
                        jpe[idx] = sPE.n
                        if idx + 2 < len(jobs):
                            load_jslice(idx + 2)
                        idx += 1
                        pev = sPE.n
                        bgc = b1c[:, e_i * 16 + j:e_i * 16 + j + 1]
                        buc = b1c[:, e_i * 16 + 8 + j:e_i * 16 + 8 + j + 1]
                        P.wait("dve", sPE, pev)
                        P.wait("dve", sDVE)
                        P.op("dve", lambda e, b=bg, c=bgc, L=wl: e.tensor_scalar(
                            gupA[:, 0:L], b.ap()[:, 0:L], c, 7.0, OP.add, OP.min), sDVE)
                        dve_g = sDVE.n
                        P.op("dve", lambda e, b=bu, c=buc, L=wl: e.tensor_scalar(
                            gupB[:, 0:L], b.ap()[:, 0:L], c, 7.0, OP.add, OP.min), sDVE)
                        guw[j % 2] = sDVE.n
                        P.wait("dve", sDVE)
                        P.op("dve", lambda e, L=wl: e.tensor_scalar_max(
                            gupB[:, 0:L], gupB[:, 0:L], -7.0), sDVE)
                        P.wait("act", sDVE, dve_g)
                        P.op("act", lambda e, L=wl: e.activation(
                            gupC[:, 0:L], gupA[:, 0:L], AF.Sigmoid, scale=1.702), sACT)
                        actv = sACT.n
                        P.wait("dve", sACT, actv)
                        P.op("dve", lambda e, L=wl: e.tensor_tensor(
                            gupC[:, 0:L], gupA[:, 0:L], gupC[:, 0:L], OP.mult), sDVE)
                        P.wait("dve", sDVE)
                        P.op("dve", lambda e, j0=j, L=wl: e.scalar_tensor_tensor(
                            ITT[j0][:, 0:L], gupB[:, 0:L], 1.0, gupC[:, 0:L],
                            OP.add, OP.mult), sDVE)
                    dve_inter = sDVE.n
                    P.wait("pe", sDVE, dve_inter)
                    P.wait("pe", cW2[eidx % 2])
                    dmark = {}
                    for gi, (gsub, half) in enumerate([(g, hh) for g in range(ngt) for hh in range(2)]):
                        gt = (w0 + gsub * 128) // 128
                        if half == 0:
                            dslot = ds_ctr[0] % 4
                            ds_ctr[0] += 1
                            P.wait("dve", cDS[dslot])
                            dslots = {}
                        dslots[gsub] = dslot
                        bank = pb[4 + gi % 2]
                        if gi >= 2:
                            P.wait("pe", sDVE, dmark[gi - 2])
                        for k in range(8):
                            P.op("pe", lambda e, b=bank, k0=k, g0=gsub, h0=half,
                                 w2s=W2B2[eidx % 2], st=(k == 0), sp=(k == 7):
                                 e.matmul(b.ap()[:, 0:512],
                                          ITT[k0][:, g0 * 128:(g0 + 1) * 128],
                                          w2s[k0][:, h0 * 512:(h0 + 1) * 512],
                                          start=st, stop=sp),
                                 sPE if k == 7 else None)
                        pev = sPE.n
                        w2pe[eidx] = pev
                        P.wait("dve", sPE, pev)
                        P.wait("dve", sDVE)
                        wv = wcs[:, gt:gt + 1]
                        P.op("dve", lambda e, b=bank, h0=half: e.tensor_tensor(
                            gupA[:, 0:512], b.ap()[:, 0:512],
                            b2r[:, h0 * 512:(h0 + 1) * 512], OP.add), sDVE)
                        P.wait("dve", sDVE)
                        P.op("dve", lambda e, h0=half, g0=dslot, w=wv: e.tensor_scalar(
                            dstage[g0][:, h0 * 512:(h0 + 1) * 512], gupA[:, 0:512],
                            w, None, OP.mult), sDVE)
                        dmark[gi] = sDVE.n
                        if half == 1:
                            dvev = sDVE.n
                            P.wait("sp", sDVE, dvev)
                            P.dma("sp", cDS[dslot], down_sc[gt * 128:(gt + 1) * 128, :],
                                  dstage[dslot])
            xtS = [vf(A3, 12288 + i * 2048, 1024) for i in range(2)]
            blk_of = {k: i for i, k in enumerate(rt["scat_keys"])}
            P.wait_all_dma("sp", cDS)
            P.wait("sp", sPE)
            P.wait("dve", sPE)
            slot_pe = [0, 0]
            pe_pi = {}
            st_pi = {}
            gl = 0
            for pi in range(16):
                pairs = [k for k in rt["scat_keys"] if k[1] == pi]
                sl = pi % 2
                if pi >= 2:
                    P.wait("sp", sPE, pe_pi[pi - 2])
                    P.wait("sp", cST[sl], st_pi[pi - 2])
                    P.wait("sp", cOUT[sl])
                P.dma("sp", cX[sl], xtS[sl], xres[pi * 128:(pi + 1) * 128, :])
                np_ = len(pairs)
                for idx, (gt, _pi) in enumerate(pairs):
                    b = blk_of[(gt, pi)]
                    s2 = gl % 2
                    gl += 1
                    P.wait("sp", sPE, slot_pe[s2])
                    P.dma("sp", cSB[s2], sblk[s2], scat[b * 128:(b + 1) * 128, :])
                    P.dma("sp", cDST[s2], dstage[s2][:, :], down_sc[gt * 128:(gt + 1) * 128, :])
                    P.wait("pe", cSB[s2])
                    P.wait("pe", cDST[s2])
                    if idx == 0:
                        P.wait("pe", sDVE)
                    for half in range(2):
                        P.op("pe", lambda e, ss=s2, h0=half, st=(idx == 0), sp=(idx == np_ - 1):
                             e.matmul(pb[h0].ap()[:, 0:512], sblk[ss],
                                      dstage[ss][:, h0 * 512:(h0 + 1) * 512],
                                      start=st, stop=sp),
                             sPE if half == 1 else None)
                    slot_pe[s2] = sPE.n
                pev = sPE.n
                pe_pi[pi] = pev
                P.wait("dve", sPE, pev)
                P.wait("dve", cX[sl])
                for half in range(2):
                    P.op("dve", lambda e, h0=half, s0=sl: e.tensor_tensor(
                        xtS[s0][:, h0 * 512:(h0 + 1) * 512], pb[h0].ap()[:, 0:512],
                        xtS[s0][:, h0 * 512:(h0 + 1) * 512], OP.add), sDVE)
                dvev = sDVE.n
                P.wait("sp", sDVE, dvev)
                P.dma("sp", cST[sl], xres[pi * 128:(pi + 1) * 128, :], xtS[sl])
                if write_out_tiles:
                    P.dma("sp", cOUT[sl], out[pi * 128:(pi + 1) * 128, :], xtS[sl])
                st_pi[pi] = cST[sl].n

        ffn_phase(False)
        if DEBUG["phase"] == "ffn1":
            write_out_from_xres()
            final_build(P)
            return nc, es

        xt_s, ce_s, xb_s = ln_scratch(16384)
        cent_cur[0], cent_cur[1] = ce_s[0], ce_s[1]
        P.wait("dve", sPE)
        P.wait("sp", sPE)
        P.wait_all_dma("sp", cST)
        dve_lnmark = {}
        for i in range(NT):
            sl = i % 2
            if i >= 2:
                P.wait("sp", sDVE, dve_lnmark[i - 2])
            P.dma("sp", cX[sl], xt_s[sl], xres[i * 128:(i + 1) * 128, :])
            P.wait("dve", cX[sl])
            ln_tile(xt_s[sl], xb_s[sl], out_bf=xb_s[sl], slot=sl)
            dve_lnmark[i] = sDVE.n
            P.wait("pe", sDVE, dve_lnmark[i])
            for c in range(8):
                transpose_blk(xb_s[sl][:, c * 128:(c + 1) * 128],
                              XT[c][:, i * 128:(i + 1) * 128], c % 2)
        act_p5a = sACT.n

        P.wait("gp", sPE)
        for c in range(8):
            P.dma("gp", gW[(2 * c) % 4], qwB[c], qw[c * 128:(c + 1) * 128, :])
            P.dma("gp", gW[(2 * c + 1) % 4], kwB[c], kw[c * 128:(c + 1) * 128, :])
        P.dma("sp", cB2, vbr_b, vb_rep)
        P.wait("pe", sACT, act_p5a)
        P.wait_all_dma("pe", gW)
        P.wait("dve", sDVE)
        P.op("dve", lambda e: e.memset(
            vB.rearrange("p (t w) -> p t w", w=65)[:, :, 64:65], 1.0), sDVE)
        dve_onesb = sDVE.n
        P.wait("pe", sDVE, dve_onesb)

        qdupP = [qdup, qdupB]
        kdupP = [kdup, kdupB]
        dup_pe_mark = [0, 0]
        qtkt_state = {}

        def emit_dup(hh):
            par = hh % 2
            P.wait_all_dma("dve", gW)
            P.wait("dve", sPE, dup_pe_mark[par])
            for j in range(2):
                P.op("dve", lambda e, j0=j, h0=hh, par0=par: e.tensor_copy(
                    qdupP[par0].rearrange("p (k j w) -> p k j w", k=8, j=2)[:, :, j0, :],
                    A2.ap()[:, 0:8192].rearrange("p (k w) -> p k w", k=8)[:, :, h0 * 64:(h0 + 1) * 64]), sDVE)
                P.op("dve", lambda e, j0=j, h0=hh, par0=par: e.tensor_copy(
                    kdupP[par0].rearrange("p (k j w) -> p k j w", k=8, j=2)[:, :, j0, :],
                    A2.ap()[:, 8192:16384].rearrange("p (k w) -> p k w", k=8)[:, :, h0 * 64:(h0 + 1) * 64]), sDVE)
            qtkt_state[("dup", hh)] = sDVE.n

        def qtkt_jobs(hh):
            hq = qb8.ap()[:, hh:hh + 1]
            hk = kb8.ap()[:, hh:hh + 1]
            return [(qdupP[hh % 2], QTa, hq, s2) for s2 in range(8)] + \
                   [(kdupP[hh % 2], KTa, hk, s2) for s2 in range(8)]

        def emit_qtkt_iter(hh, it, jobsL, bank, ev_wait_mark):
            dup, dstT, bia, sc = jobsL[it]
            if it == 0:
                P.wait("pe", sDVE, qtkt_state[("dup", hh)])
            P.wait("pe", sDVE, ev_wait_mark)
            for k in range(8):
                P.op("pe", lambda e, b=bank, k0=k, s0=sc, dp=dup,
                     st=(k == 0), sp=(k == 7):
                     e.matmul(b.ap()[:, 0:512], dp[:, k0 * 128:(k0 + 1) * 128],
                              XT[k0][:, s0 * 512:(s0 + 1) * 512], start=st, stop=sp),
                     sPE if k == 7 else None)
            pev = sPE.n
            dup_pe_mark[hh % 2] = pev
            P.wait("act", sPE, pev)
            P.op("act", lambda e, b=bank, s0=sc, d=dstT, bi=bia: e.activation(
                d[64:128, s0 * 512:(s0 + 1) * 512], b.ap()[64:128, 0:512], AF.Square,
                bias=bi[64:128, :], scale=0.125), sACT)
            actv2 = sACT.n
            P.wait("dve", sACT, actv2)
            P.op("dve", lambda e, b=bank, s0=sc, d=dstT, bi=bia: e.tensor_scalar(
                d[0:64, s0 * 512:(s0 + 1) * 512], b.ap()[0:64, 0:512], 0.125,
                bi[0:64, :], OP.mult, OP.add), sDVE)
            return sDVE.n

        P.wait("gp", sPE)
        for c in range(8):
            P.dma("gp", gV[0], vwbuf[0][:, c * 64:(c + 1) * 64],
                  vw[c * 128:(c + 1) * 128, 0:64])
        emit_dup(0)
        jl0 = qtkt_jobs(0)
        qk_ev = {-2: 0, -1: 0}
        for it in range(16):
            qk_ev[it] = emit_qtkt_iter(0, it, jl0, pb[it % 2], qk_ev[it - 2])
        qtkt_state[("done_dve", 0)] = sDVE.n
        qtkt_state[("done_act", 0)] = sACT.n
        P.wait("pe", sDVE)

        for h in range(B_HEADS):
            P.wait("act", sPE)
            P.wait("act", cTB)
            if h + 1 < B_HEADS:
                P.wait("gp", sPE)
                for c in range(8):
                    P.dma("gp", gV[(h + 1) % 2], vwbuf[(h + 1) % 2][:, c * 64:(c + 1) * 64],
                          vw[c * 128:(c + 1) * 128, (h + 1) * 64:(h + 2) * 64])
            P.wait("pe", sDVE, qtkt_state[("done_dve", h)])
            P.wait("pe", sACT, qtkt_state[("done_act", h)])
            fm = {}
            fjobs = ([("qp", mc, sc) for mc in range(2) for sc in range(8)] +
                     [("kp", mc, sc) for mc in range(2) for sc in range(8)] +
                     [("kn", 0, t) for t in range(NT)])
            for it, (kind, mc, sc) in enumerate(fjobs):
                bank = pb[it % 2]
                if it >= 2:
                    P.wait("pe", sACT, fm[it - 2])
                if kind == "qp":
                    P.op("pe", lambda e, b=bank, m0=mc, s0=sc:
                         e.matmul(b.ap()[:, 0:512], projT_aug.ap()[:, m0 * 128:(m0 + 1) * 128],
                                  QTa[:, s0 * 512:(s0 + 1) * 512], start=True, stop=True), sPE)
                elif kind == "kp":
                    P.op("pe", lambda e, b=bank, m0=mc, s0=sc:
                         e.matmul(b.ap()[:, 0:512], projT_aug.ap()[:, m0 * 128:(m0 + 1) * 128],
                                  KTa[:, s0 * 512:(s0 + 1) * 512], start=True, stop=True), sPE)
                else:
                    P.op("pe", lambda e, b=bank, t0=sc:
                         e.matmul(b.ap()[:, 0:256], KTa[:, t0 * 128:(t0 + 1) * 128],
                                  projT_aug.ap()[:, :], start=True, stop=True), sPE)
                pev = sPE.n
                P.wait("act", sPE, pev)
                if kind == "qp":
                    P.op("act", lambda e, b=bank, m0=mc, s0=sc: e.activation(
                        qpT[m0][:, s0 * 512:(s0 + 1) * 512], b.ap()[:, 0:512], AF.Exp), sACT)
                elif kind == "kp":
                    P.op("act", lambda e, b=bank, m0=mc, s0=sc: e.activation(
                        kpT[m0][:, s0 * 512:(s0 + 1) * 512], b.ap()[:, 0:512], AF.Exp), sACT)
                else:
                    P.op("act", lambda e, b=bank, t0=sc: e.activation(
                        kpN[:, t0 * 256:(t0 + 1) * 256], b.ap()[:, 0:256], AF.Exp), sACT)
                fm[it] = sACT.n
            P.wait("pe", sACT)
            P.wait("pe", gV[h % 2])
            if h == 0:
                P.wait("dve", cB2)
            vbm = {}
            for t in range(NT):
                bank = pb[t % 2]
                if t >= 2:
                    P.wait("pe", sDVE, vbm[t - 2])
                for k in range(8):
                    P.op("pe", lambda e, b=bank, k0=k, t0=t, hh=(h % 2),
                         st=(k == 0), sp=(k == 7):
                         e.matmul(b.ap()[:, 0:64], XT[k0][:, t0 * 128:(t0 + 1) * 128],
                                  vwbuf[hh][:, k0 * 64:(k0 + 1) * 64], start=st, stop=sp),
                         sPE if k == 7 else None)
                pev = sPE.n
                P.wait("dve", sPE, pev)
                P.op("dve", lambda e, b=bank, t0=t, h0=h: e.tensor_tensor(
                    vB[:, t0 * 65:t0 * 65 + 64], b.ap()[:, 0:64],
                    vbr_b[:, h0 * 64:(h0 + 1) * 64], OP.add), sDVE)
                vbm[t] = sDVE.n
            P.wait("dve", sDVE)
            P.op("dve", lambda e: e.memset(kvf32[0], 0.0), sDVE)
            P.op("dve", lambda e: e.memset(kvf32[1], 0.0), sDVE)
            dve_v = sDVE.n
            P.wait("pe", sDVE, dve_v)
            if h + 1 < B_HEADS:
                emit_dup(h + 1)
                jlN = qtkt_jobs(h + 1)
                qk_it = [0]
                qk_ev2 = {-1: 0}

            atm_dve = [0, 0]
            attn_dve = 0
            kv_act = 0
            kv_dve = 0
            attn_dve2 = [0, 0]
            for cch in range(NT):
                bA = pb[cch % 2]
                bN = pb[2 if cch % 2 == 0 else 4]
                P.wait("pe", sDVE, atm_dve[cch % 2])
                for mc in range(2):
                    P.op("pe", lambda e, b=bA, m0=mc, c0=cch, st=(mc == 0), sp=(mc == 1):
                         e.matmul(b.ap()[:, 0:128], kpT[m0][:, c0 * 128:(c0 + 1) * 128],
                                  qpT[m0][:, c0 * 128:(c0 + 1) * 128], start=st, stop=sp),
                         sPE if mc == 1 else None)
                peA = sPE.n
                P.wait("dve", sPE, peA)
                P.op("dve", lambda e, b=bA, c0=cch: e.tensor_tensor(
                    atm[c0 % 2], b.ap()[:, 0:128], tri_sb.ap()[:, :], OP.mult), sDVE)
                atm_dve[cch % 2] = sDVE.n
                P.wait("pe", sDVE, atm_dve[cch % 2])
                P.wait("pe", sDVE, attn_dve2[cch % 2])
                if cch > 0:
                    P.wait("pe", sACT, kv_act)
                    for mc in range(2):
                        P.op("pe", lambda e, b=bN, m0=mc, c0=cch, kv=kvsb[(cch - 1) % 2]:
                             e.matmul(b.ap()[:, 0:65], qpT[m0][:, c0 * 128:(c0 + 1) * 128],
                                      kv[m0], start=(m0 == 0), stop=False), None)
                    P.op("pe", lambda e, b=bN, c0=cch:
                         e.matmul(b.ap()[:, 0:65], atm[c0 % 2],
                                  vB[:, c0 * 65:(c0 + 1) * 65], start=False, stop=True), sPE)
                else:
                    P.op("pe", lambda e, b=bN, c0=cch:
                         e.matmul(b.ap()[:, 0:65], atm[c0 % 2],
                                  vB[:, c0 * 65:(c0 + 1) * 65], start=True, stop=True), sPE)
                peN = sPE.n
                num_pe = peN
                if cch > 0:
                    P.wait("pe", sDVE, kv_dve)
                for mc in range(2):
                    P.op("pe", lambda e, m0=mc, c0=cch:
                         e.matmul(pb[3].ap()[:, m0 * 65:(m0 + 1) * 65],
                                  kpN[:, c0 * 256 + m0 * 128:c0 * 256 + (m0 + 1) * 128],
                                  vB[:, c0 * 65:(c0 + 1) * 65],
                                  start=(m0 == 0), stop=(m0 == 1)),
                         sPE if mc == 1 else None)
                peU = sPE.n
                if h + 1 < B_HEADS and cch % 2 == 1 and qk_it[0] < 16:
                    it2 = qk_it[0]
                    qk_ev2[it2] = emit_qtkt_iter(h + 1, it2, jlN, pb[5], qk_ev2[it2 - 1])
                    qk_it[0] += 1
                if cch < NT - 1:
                    P.wait("dve", sPE, peU)
                    P.wait("dve", sDVE)
                    for mc in range(2):
                        P.op("dve", lambda e, m0=mc: e.tensor_tensor(
                            kvf32[m0], kvf32[m0], pb[3].ap()[:, m0 * 65:(m0 + 1) * 65],
                            OP.add), sDVE)
                    kv_dve = sDVE.n
                    P.wait("act", sDVE, kv_dve)
                    P.wait("act", sPE, num_pe)
                    P.op("act", lambda e, kv=kvsb[cch % 2]: e.copy(kv[0], kvf32[0]), sACT)
                    P.op("act", lambda e, kv=kvsb[cch % 2]: e.copy(kv[1], kvf32[1]), sACT)
                    kv_act = sACT.n
                P.wait("dve", sPE, peN)
                P.op("dve", lambda e, b=bN: e.reciprocal(col["rcp"], b.ap()[:, 64:65]), sDVE)
                P.wait("dve", sDVE)
                P.op("dve", lambda e, b=bN: e.tensor_scalar(
                    battn, b.ap()[:, 0:64], col["rcp"], None, OP.mult), sDVE)
                attn_dve2[cch % 2] = sDVE.n
                P.wait("pe", sDVE, attn_dve2[cch % 2])
                transpose_blk(battn, atT[0:64, cch * 128:(cch + 1) * 128],
                              cch % 2, width=64, p_in=128)
                P.wait("dve", sPE)
            if h + 1 < B_HEADS:
                qtkt_state[("done_dve", h + 1)] = sDVE.n
                qtkt_state[("done_act", h + 1)] = sACT.n
            act_scan = sACT.n
            P.wait("sp", sACT, act_scan)
            P.dma("sp", cTB, attnTb[h * 64:(h + 1) * 64, :], atT)

        P.wait("gp", sPE)
        for c in range(8):
            P.dma("gp", gW[c % 4], owB[c], ow[c * 128:(c + 1) * 128, :])
        P.wait("sp", sDVE)
        P.wait("sp", sPE)
        P.dma("sp", cC, obrf, ob_rep)
        P.dma("sp", cC, gmrf, gam_rep)
        P.dma("sp", cC, btrf, bet_rep)
        xt5 = [vf(A3, 12288 + i * 2048, 1024) for i in range(2)]
        ce5 = [vf(A3, 16384 + i * 2048, 1024) for i in range(2)]
        xb5 = [v1(A3, 20480 + i * 1024, 1024) for i in range(2)]
        cent_cur[0], cent_cur[1] = ce5[0], ce5[1]
        P.wait_all_dma("pe", gW)
        pe_t5 = {}
        st_t5 = {}
        dv_t5 = {}
        for i in range(NT):
            sl = i % 2
            if i >= 2:
                P.wait("sp", sPE, pe_t5[i - 2])
                P.wait("sp", cST[sl], st_t5[i - 2])
                P.wait("sp", cOUT[sl])
            if i < 2:
                P.wait("sp", sACT)
                P.wait("sp", sPE)
                P.wait("sp", cTB)
            P.dma("sp", cSG[sl], bstage[sl].rearrange("p (c w) -> p c w", c=8),
                  attnTb.rearrange("(c p) s -> c p s", p=128)[:, :, i * 128:(i + 1) * 128]
                  .transpose([1, 0, 2]))
            P.dma("sp", cX[sl], xt5[sl], xres[i * 128:(i + 1) * 128, :])
            P.wait("pe", cSG[sl])
            P.wait("pe", cX[sl])
            if i >= 2:
                P.wait("pe", sDVE, dv_t5[i - 2])
            for half in range(2):
                bank = pb[half + 2 * sl]
                for c in range(8):
                    P.op("pe", lambda e, b=bank, c0=c, h0=half, s0=sl,
                         st=(c == 0), sp=(c == 7):
                         e.matmul(b.ap()[:, 0:512], bstage[s0][:, c0 * 128:(c0 + 1) * 128],
                                  owB[c0][:, h0 * 512:(h0 + 1) * 512], start=st, stop=sp),
                         sPE if c == 7 else None)
            pev = sPE.n
            pe_t5[i] = pev
            P.wait("dve", cX[sl])
            if i == 0:
                P.wait("dve", cC)
            ln_tile(xt5[sl], xb5[sl], out_f32=xn2f[sl], slot=sl, eps=LN_EPS)
            P.wait("dve", sPE, pev)
            P.wait("dve", sDVE)
            for half in range(2):
                P.op("dve", lambda e, h0=half, s0=sl: e.tensor_tensor(
                    residf[s0][:, h0 * 512:(h0 + 1) * 512], pb[h0 + 2 * s0].ap()[:, 0:512],
                    obrf[:, h0 * 512:(h0 + 1) * 512], OP.add), sDVE)
            P.wait("dve", sDVE)
            dv_t5[i] = sDVE.n
            P.op("dve", lambda e, s0=sl: e.tensor_tensor(residf[s0], residf[s0], xn2f[s0], OP.add), sDVE)
            ln_tile(residf[sl], xb5[sl], out_f32=xn2f[sl], eps=LN2_EPS, slot=sl,
                    gamma_beta=(gmrf, btrf))
            P.wait("dve", sDVE)
            P.op("dve", lambda e, s0=sl: e.tensor_tensor(xt5[s0], xt5[s0], xn2f[s0], OP.add), sDVE)
            dvev = sDVE.n
            P.wait("sp", sDVE, dvev)
            P.dma("sp", cST[sl], xres[i * 128:(i + 1) * 128, :], xt5[sl])
            P.dma("sp", cOUT[sl], out[i * 128:(i + 1) * 128, :], xt5[sl])
            st_t5[i] = cST[sl].n

        if DEBUG["phase"] == "attn_b":
            P.wait_all_dma("sp", SP_SEMS)
            P.wait_all_dma("gp", GP_SEMS)
            final_build(P)
            return nc, es

        ffn_phase(True)
        P.wait_all_dma("sp", SP_SEMS)
        P.wait_all_dma("gp", GP_SEMS)
        final_build(P)
        return nc, es


def final_build(P):
    nc = P.nc
    with nc.Block() as blk:
        @blk.gpsimd
        def _(g):
            for f in P.q["gp"]:
                f(g)

        @blk.sync
        def _(s):
            for f in P.q["sp"]:
                f(s)

        @blk.tensor
        def _(t):
            for f in P.q["pe"]:
                f(t)

        @blk.scalar
        def _(sc):
            for f in P.q["act"]:
                f(sc)

        @blk.vector
        def _(v):
            for f in P.q["dve"]:
                f(v)


def prep_inputs(inputs):
    rt = build_routing(inputs["L0_ffn_router_indices"], inputs["L0_ffn_routing_weights"])
    f32 = np.float32
    cm = {}
    cm["identity"] = np.eye(128, dtype=np.float16)
    cm["trimask"] = np.triu(np.ones((128, 128), dtype=np.float32)).astype(np.float16)
    for nm, key in (("Wq", "L0_attn_a_Wq"), ("Wkv", "L0_attn_a_Wkv_down"),
                    ("Wku", "L0_attn_a_Wk_up"), ("Wvu", "L0_attn_a_Wv_up"),
                    ("Wo", "L0_attn_a_Wo"), ("qw", "L0_attn_b_q_w"),
                    ("kw", "L0_attn_b_k_w"), ("vw", "L0_attn_b_v_w"),
                    ("ow", "L0_attn_b_out_w"), ("W2", "L0_ffn_down_proj")):
        cm[nm] = np.ascontiguousarray(np.asarray(inputs[key], dtype=f32))
    qb = np.asarray(inputs["L0_attn_b_q_b"], dtype=f32)
    kb = np.asarray(inputs["L0_attn_b_k_b"], dtype=f32)
    cm["qbdup"] = np.ascontiguousarray(
        np.stack([np.concatenate([qb[h * 64:(h + 1) * 64]] * 2) for h in range(16)], axis=1))
    cm["kbdup"] = np.ascontiguousarray(
        np.stack([np.concatenate([kb[h * 64:(h + 1) * 64]] * 2) for h in range(16)], axis=1))
    for nm, key in (("vb_rep", "L0_attn_b_v_b"), ("ob_rep", "L0_attn_b_out_b"),
                    ("gam_rep", "L0_attn_b_gamma"), ("bet_rep", "L0_attn_b_beta")):
        cm[nm] = np.ascontiguousarray(np.broadcast_to(np.asarray(inputs[key], dtype=f32), (128, D)))
    proj = np.asarray(inputs["L0_attn_b_proj"], dtype=f32)
    p4 = np.full((128, 256), -0.5, dtype=f32)
    p4[0:64, :] = proj.T
    cm["proj4"] = np.ascontiguousarray(p4)
    W1 = np.asarray(inputs["L0_ffn_gate_up_proj"], dtype=f32)
    perm = np.concatenate([np.arange(0, 2048, 2), np.arange(1, 2048, 2)])
    permj = np.concatenate([np.concatenate([perm[j * 128:(j + 1) * 128],
                                            perm[1024 + j * 128:1024 + (j + 1) * 128]])
                            for j in range(8)])
    cm["W1p"] = np.ascontiguousarray(W1[:, :, permj])
    b1 = np.asarray(inputs["L0_ffn_gate_up_proj_bias"], dtype=f32)[:, perm]
    cm["b1cols"] = np.ascontiguousarray(
        np.concatenate([b1[e].reshape(16, 128).T for e in range(N_EXP)], axis=1))
    cm["b2rep"] = np.ascontiguousarray(
        np.broadcast_to(np.asarray(inputs["L0_ffn_down_proj_bias"], dtype=f32)[:, None, :],
                        (N_EXP, 128, D)))
    cm["PTg"] = rt["PT"]
    cm["scat"] = np.ascontiguousarray(rt["scat_blocks"].reshape(-1, 128))
    return rt, cm


def kernel(**inputs):
    rt, cm = prep_inputs(inputs)
    nc, es = build_program(rt)
    x = np.ascontiguousarray(np.asarray(inputs["L0_attn_a_x"], dtype=np.float32))
    in_maps = []
    for c in range(NCORES):
        m = dict(cm)
        m["x"] = np.ascontiguousarray(x[c])
        m["wcol"] = rt["wcol"] if c == 0 else np.zeros_like(rt["wcol"])
        in_maps.append(m)
    res = run_bass_kernel_spmd(nc, in_maps, core_ids=list(range(NCORES)))
    return np.stack([res.results[c]["out"] for c in range(NCORES)], axis=0)


# version 08
# speedup vs baseline: 1.0844x; 1.0844x over previous
import numpy as np
import ml_dtypes
from contextlib import ExitStack
import concourse.bass as bass
import concourse.mybir as mybir
from concourse.bass_utils import run_bass_kernel_spmd

F32 = mybir.dt.float32
F16 = mybir.dt.float16
AX = mybir.AxisListType.X
AF = mybir.ActivationFunctionType
OP = mybir.AluOpType

NCORES = 8
T = 4096
D = 1024
NT = T // 128
A_HEADS = 8
B_HEADS = 16
LN2_EPS = 1e-6
LN_EPS = 1e-5
N_EXP = 8

DEBUG = {"phase": "all"}

ENGINES = ("pe", "act", "dve", "sp", "gp")


class Sem:
    def __init__(self, h):
        self.h = h
        self.n = 0
        self.eng = None

    def inc(self, k=1):
        self.n += k
        return self.n


class Plan:
    def __init__(self, nc, es):
        self.nc = nc
        self.es = es
        self.q = {e: [] for e in ENGINES}

    def sem(self, name):
        return Sem(self.es.enter_context(self.nc.semaphore(name)))

    def emit(self, eng, f):
        self.q[eng].append(f)

    def wait(self, eng, sem, val=None):
        v = sem.n if val is None else val
        if v > 0:
            self.emit(eng, lambda e, h=sem.h, vv=v: e.wait_ge(h, vv))

    def op(self, eng, fn, sem=None, k=1):
        if sem is None:
            self.emit(eng, fn)
            return None
        v = sem.inc(k)
        self.emit(eng, lambda e, f=fn, h=sem.h, kk=k: f(e).then_inc(h, kk))
        return v

    def dma(self, eng, sem, out_ap, in_ap):
        assert sem.eng in (None, eng), "sem shared across queues"
        sem.eng = eng
        if sem.n > 0:
            self.wait(eng, sem)
        return self.op(eng, lambda e, o=out_ap, i=in_ap: e.dma_start(out=o, in_=i), sem, 16)

    def wait_all_dma(self, eng, sems):
        for sm in sems:
            self.wait(eng, sm)


def ceil128(n):
    return ((n + 127) // 128) * 128


def build_routing(router_indices, routing_weights):
    ri = np.asarray(router_indices)
    rw = np.asarray(routing_weights)
    ntok, topk = ri.shape
    per_e = [[] for _ in range(N_EXP)]
    for t in range(ntok):
        for j in range(topk):
            per_e[int(ri[t, j])].append((t, j))
    caps = [ceil128(len(sl)) for sl in per_e]
    offs = np.cumsum([0] + caps)
    G = int(offs[-1])
    GT = max(G // 128, 1)
    g_pos = np.zeros(max(G, 1), dtype=np.int64)
    g_w = np.zeros(max(G, 1), dtype=np.float32)
    g_real = np.zeros(max(G, 1), dtype=bool)
    segs = []
    for e in range(N_EXP):
        sl = sorted(per_e[e], key=lambda tj: (tj[0] // 128, tj[0], tj[1]))
        g0e = int(offs[e])
        for i, (t, j) in enumerate(sl):
            g = g0e + i
            g_pos[g] = t
            g_w[g] = rw[t, int(ri[t, j])]
            g_real[g] = True
        i = 0
        n = len(sl)
        while i < n:
            tau = sl[i][0] // 128
            i1 = i
            while i1 < n and sl[i1][0] // 128 == tau:
                i1 += 1
            segs.append([e, tau, g0e + i, g0e + i1])
            i = i1
        if caps[e] > n and n > 0:
            segs[-1][3] = g0e + caps[e]
    PT = np.zeros((128, max(G, 1)), dtype=np.float16)
    for e, tau, ga, gb in segs:
        for g in range(ga, gb):
            if g_real[g]:
                PT[int(g_pos[g]) % 128, g] = 1
    windows = []
    for e in range(N_EXP):
        a, b = int(offs[e]), int(offs[e]) + caps[e]
        w0 = a
        while w0 < b:
            w1 = min(w0 + 512, b)
            wsegs = [(tau, max(ga, w0), min(gb, w1))
                     for (ee, tau, ga, gb) in segs if ee == e and gb > w0 and ga < w1]
            windows.append((e, w0, w1, wsegs))
            w0 = w1
    pair_map = {}
    for g in range(G):
        if not g_real[g]:
            continue
        key = (g // 128, int(g_pos[g]) // 128)
        if key not in pair_map:
            pair_map[key] = np.zeros((128, 128), dtype=np.float32)
        pair_map[key][g % 128, int(g_pos[g]) % 128] += 1.0
    scat_keys = sorted(pair_map.keys(), key=lambda k: (k[1], k[0]))
    if scat_keys:
        scat_blocks = np.stack([pair_map[k] for k in scat_keys], axis=0).astype(np.float16)
    else:
        scat_blocks = np.zeros((1, 128, 128), dtype=np.float16)
    wcol = np.zeros((128, GT), dtype=np.float32)
    for g in range(G):
        wcol[g % 128, g // 128] = g_w[g]
    assert G <= 5120, G
    return dict(G=G, GT=GT, caps=caps, offs=offs, segs=segs, windows=windows,
                PT=np.ascontiguousarray(PT), scat_keys=scat_keys,
                scat_blocks=np.ascontiguousarray(scat_blocks), wcol=wcol)


def build_program(rt):
    nc = bass.Bass("TRN2", target_bir_lowering=False)
    G, GT = rt["G"], rt["GT"]
    NBLK = len(rt["scat_keys"])

    din = lambda n, sh, dt=F32: nc.dram_tensor(n, sh, dt, kind="ExternalInput").ap()
    x_in = din("x", [T, D])
    Wq = din("Wq", [2048, D])
    Wkv = din("Wkv", [64, D])
    Wku = din("Wku", [2048, 64])
    Wvu = din("Wvu", [2048, 64])
    Wo = din("Wo", [D, 2048])
    qw = din("qw", [D, D]); kw = din("kw", [D, D]); vw = din("vw", [D, D]); ow = din("ow", [D, D])
    qbdup = din("qbdup", [128, 16]); kbdup = din("kbdup", [128, 16])
    vb_rep = din("vb_rep", [128, D]); ob_rep = din("ob_rep", [128, D])
    gam_rep = din("gam_rep", [128, D]); bet_rep = din("bet_rep", [128, D])
    proj4 = din("proj4", [128, 256])
    W1p = din("W1p", [N_EXP, D, 2048])
    W2 = din("W2", [N_EXP, D, D])
    b1cols = din("b1cols", [128, 16 * N_EXP])
    b2rep = din("b2rep", [N_EXP, 128, D])
    identity = din("identity", [128, 128], F16)
    trimask = din("trimask", [128, 128], F16)
    PTg = din("PTg", [128, max(G, 1)], F16)
    scat = din("scat", [max(NBLK, 1) * 128, 128], F16)
    wcol_in = din("wcol", [128, GT])
    out = nc.dram_tensor("out", [T, D], F32, kind="ExternalOutput").ap()

    dsc = lambda n, sh, dt=F32: nc.dram_tensor(n, sh, dt, kind="Internal").ap()
    xres = dsc("xres", [T, D])
    attnTa = dsc("attnTa", [16 * 128, T], F16)
    attnTb = dsc("attnTb", [B_HEADS * 64, T], F16)
    down_sc = dsc("down_sc", [GT * 128, D], F16)
    W1c = dsc("W1c", [N_EXP * 1024, 2048], F16)
    W2c = dsc("W2c", [N_EXP * 1024, 1024], F16)
    dbg = nc.dram_tensor("dbg", [6 * 128, T], F16, kind="Internal").ap()

    es = ExitStack()
    with es:
        A1 = es.enter_context(nc.sbuf_tensor("A1", [128, 32768], F16))
        A2 = es.enter_context(nc.sbuf_tensor("A2", [128, 24576], F16))
        A3 = es.enter_context(nc.sbuf_tensor("A3", [128, 40960], F16))
        id_sb = es.enter_context(nc.sbuf_tensor("id_sb", [128, 128], F16))
        tri_sb = es.enter_context(nc.sbuf_tensor("tri_sb", [128, 128], F16))
        projT_aug = es.enter_context(nc.sbuf_tensor("projT_aug", [128, 256], F16))
        qb8 = es.enter_context(nc.sbuf_tensor("qb8", [128, 16], F32))
        kb8 = es.enter_context(nc.sbuf_tensor("kb8", [128, 16], F32))
        colT = es.enter_context(nc.sbuf_tensor("colT", [128, 16], F32))
        pb = [es.enter_context(nc.psum_tensor(f"pb{i}", [128, 512], F32)) for i in range(6)]
        pt = [es.enter_context(nc.psum_tensor(f"pt{i}", [128, 512], F16)) for i in range(2)]

        def v1(ar, c0, cols):
            return ar.ap()[:, c0:c0 + cols]

        def vf(ar, c0, cols_f32):
            return ar.ap().bitcast(F32)[:, c0 // 2:c0 // 2 + cols_f32]

        col = {n: colT.ap()[:, i:i + 1] for i, n in enumerate(
            ("sum", "mean", "var", "rstd", "rcp", "s2", "m2", "v2", "r2", "rc2"))}

        XT = [v1(A1, i * 4096, 4096) for i in range(8)]
        xn_ffn = [v1(A1, t * 1024, 1024) for t in range(16)]
        ITT = [v1(A1, 16384 + j * 512, 512) for j in range(8)]
        gupA = vf(A1, 20480, 512)
        gupB = vf(A1, 21504, 512)
        gupC = vf(A1, 22528, 512)
        dstage = [v1(A1, 23552 + d * 1024, 1024) for d in range(4)]
        b2r = vf(A1, 27648, 1024)
        sblk = [v1(A1, 29696 + i * 128, 128) for i in range(2)]
        b1c = vf(A1, 29952, 128)
        wcs = vf(A1, 30208, GT)
        ptw = [v1(A1, 30336 + i * 512, 512) for i in range(2)]
        WqT = [v1(A2, r * 2048, 2048) for r in range(8)]
        kuT = [A2.ap()[0:64, 16384 + c * 128:16384 + (c + 1) * 128] for c in range(16)]
        vu_h = [A2.ap()[0:64, 18432 + h * 256:18432 + (h + 1) * 256] for h in range(8)]
        wkvT = [v1(A2, 20480 + c * 64, 64) for c in range(8)]
        woT = [v1(A2, c * 1024, 1024) for c in range(16)]
        qwB = [v1(A2, c * 1024, 1024) for c in range(8)]
        kwB = [v1(A2, 8192 + c * 1024, 1024) for c in range(8)]
        vwbuf = [v1(A2, 16384 + i * 512, 512) for i in range(2)]
        qdup = v1(A2, 17408, 1024)
        kdup = v1(A2, 18432, 1024)
        vbr_b = vf(A2, 19456, 1024)
        owB = [v1(A2, c * 1024, 1024) for c in range(8)]
        W1J = [v1(A2, sl * 2048, 2048) for sl in range(2)]
        W2B2 = [[v1(A2, 4096 + sl * 8192 + k * 1024, 1024) for k in range(8)] for sl in range(2)]
        qT = [v1(A3, mc * 4096, 4096) for mc in range(2)]
        kT = [v1(A3, 8192 + mc * 4096, 4096) for mc in range(2)]
        vA = v1(A3, 16384, 8224)
        aoT = [v1(A3, 24608 + mc * 4096, 4096) for mc in range(2)]
        latT = A3.ap()[0:64, 32800:36896]
        expS = [v1(A3, 36896 + i * 512, 512) for i in range(2)]
        attn_nat = v1(A3, 37920, 256)
        QTa = v1(A3, 0, 4096)
        KTa = v1(A3, 4096, 4096)
        qpT = [v1(A3, 8192 + mc * 4096, 4096) for mc in range(2)]
        kpT = [v1(A3, 16384 + mc * 4096, 4096) for mc in range(2)]
        kpN = v1(A3, 24576, 8192)
        vB = v1(A3, 32768, 2080)
        atT = A3.ap()[0:64, 34848:38944]
        kvsb = [[v1(A3, 38944 + (p * 2 + i) * 65, 65) for i in range(2)] for p in range(2)]
        atm = [v1(A3, 39204 + i * 128, 128) for i in range(2)]
        kvf32 = [vf(A3, 39464 + i * 264, 65) for i in range(2)]
        battn = v1(A3, 39996, 64)
        stage3 = [v1(A3, mc * 2048, 2048) for mc in range(2)]
        xt3 = [vf(A3, 8192 + i * 2048, 1024) for i in range(2)]
        bstage = [v1(A3, i * 1024, 1024) for i in range(2)]
        xn2f = [vf(A3, 2048 + i * 2048, 1024) for i in range(2)]
        residf = [vf(A3, 22528 + i * 2048, 1024) for i in range(2)]
        obrf = vf(A3, 6144, 1024)
        gmrf = vf(A3, 8192, 1024)
        btrf = vf(A3, 10240, 1024)
        XgT = [v1(A3, k * max(G, 1), max(G, 1)) for k in range(8)]

        def ln_scratch(base):
            xs = [vf(A3, base + i * 2048, 1024) for i in range(2)]
            cs = [vf(A3, base + 4096 + i * 2048, 1024) for i in range(2)]
            xb = [v1(A3, base + 8192 + i * 1024, 1024) for i in range(2)]
            return xs, cs, xb

        P = Plan(nc, es)
        sPE = P.sem("sPE")
        sACT = P.sem("sACT")
        sDVE = P.sem("sDVE")
        cC = P.sem("cC")
        cX = [P.sem(f"cX{i}") for i in range(2)]
        cSG = [P.sem(f"cSG{i}") for i in range(2)]
        cST = [P.sem(f"cST{i}") for i in range(2)]
        cOUT = [P.sem(f"cOUT{i}") for i in range(2)]
        cPTW = [P.sem(f"cPTW{i}") for i in range(2)]
        cSB = [P.sem(f"cSB{i}") for i in range(2)]
        cDST = [P.sem(f"cDST{i}") for i in range(2)]
        cDS = [P.sem(f"cDS{i}") for i in range(4)]
        cB2 = P.sem("cB2")
        cTA = [P.sem(f"cTA{i}") for i in range(2)]
        cTB = P.sem("cTB")
        cXR = P.sem("cXR")
        gW = [P.sem(f"gW{i}") for i in range(4)]
        cW1 = [P.sem(f"cW1{i}") for i in range(2)]
        cW2 = [P.sem(f"cW2{i}") for i in range(2)]
        gV = [P.sem(f"gV{i}") for i in range(2)]
        SP_SEMS = [cC] + cX + cSG + cST + cOUT + cPTW + cSB + cDST + cDS + [cB2] + cTA + [cTB, cXR] + cW1 + cW2
        GP_SEMS = gW + gV

        pt_last = [0, 0]
        ln_state = {0: 0, 1: 0}
        cent_cur = [None, None]

        def transpose_blk(src_slice, dst_slice, bank, width=128, p_in=128):
            ptv = pt[bank].ap()[0:width, 0:p_in]
            P.wait("pe", sACT, pt_last[bank])
            P.op("pe", lambda e, s=src_slice, p=ptv, ii=id_sb.ap()[0:p_in, 0:p_in]: e.transpose(p, s, ii), sPE)
            pev = sPE.n
            P.wait("act", sPE, pev)
            P.op("act", lambda e, p=ptv, d=dst_slice: e.copy(d, p), sACT)
            pt_last[bank] = sACT.n
            return sACT.n

        def ln_tile(src_f32, sq_out_bf, out_bf=None, out_f32=None, eps=LN_EPS, slot=0,
                    gamma_beta=None, ld=()):
            cs, cm, cv, cr, crcp = ((col["sum"], col["mean"], col["var"], col["rstd"], col["rcp"])
                                    if slot == 0 else
                                    (col["s2"], col["m2"], col["v2"], col["r2"], col["rc2"]))
            P.wait("dve", sACT, ln_state[slot])
            P.wait("dve", sDVE)
            for sm in ld:
                P.wait("act", sm)
            P.op("dve", lambda e, s=src_f32, c=cs: e.reduce_sum(c, s, AX), sDVE)
            dve_sum = sDVE.n
            P.op("act", lambda e, s=src_f32, v=cv, q=sq_out_bf: e.activation(
                q, s, AF.Square, accum_out=v), sACT)
            ln_state[slot] = sACT.n
            P.wait("dve", sACT, ln_state[slot])
            P.wait("dve", sDVE, dve_sum)
            P.op("dve", lambda e, c=cs, m=cm: e.tensor_scalar_mul(m, c, 1.0 / D), sDVE)
            P.wait("dve", sDVE)
            P.op("dve", lambda e, m=cm, c=cs: e.tensor_tensor(c, m, m, OP.mult), sDVE)
            P.wait("dve", sDVE)
            P.op("dve", lambda e, v=cv, c=cs: e.tensor_scalar(
                v, v, 1.0 / D, None, OP.mult), sDVE)
            P.wait("dve", sDVE)
            P.op("dve", lambda e, v=cv, c=cs: e.scalar_tensor_tensor(
                v, v, eps, c, OP.add, OP.subtract), sDVE)
            P.wait("dve", sDVE)
            P.op("dve", lambda e, v=cv, r=crcp: e.reciprocal(r, v), sDVE)
            dve_rcp = sDVE.n
            P.wait("act", sDVE, dve_rcp)
            P.op("act", lambda e, r=crcp, o=cr: e.activation(o, r, AF.Sqrt), sACT)
            act_rstd = sACT.n
            P.wait("dve", sACT, act_rstd)
            P.op("dve", lambda e, m=cm, r=cr: e.tensor_tensor(m, m, r, OP.mult), sDVE)
            P.wait("dve", sDVE)
            if gamma_beta is None:
                if out_bf is not None:
                    P.op("dve", lambda e, s=src_f32, r=cr, m=cm, o=out_bf: e.tensor_scalar(
                        o, s, r, m, OP.mult, OP.subtract), sDVE)
                if out_f32 is not None:
                    P.op("dve", lambda e, s=src_f32, r=cr, m=cm, o=out_f32: e.tensor_scalar(
                        o, s, r, m, OP.mult, OP.subtract), sDVE)
            else:
                g_rep, b_rep = gamma_beta
                P.op("dve", lambda e, s=src_f32, r=cr, m=cm, o=out_f32: e.tensor_scalar(
                    o, s, r, m, OP.mult, OP.subtract), sDVE)
                P.wait("dve", sDVE)
                P.op("dve", lambda e, g=g_rep, o=out_f32: e.scalar_tensor_tensor(
                    o, o, 1.0, g, OP.mult, OP.mult), sDVE)
                P.wait("dve", sDVE)
                P.op("dve", lambda e, b=b_rep, o=out_f32: e.tensor_tensor(
                    o, o, b, OP.add), sDVE)
            return sDVE.n

        for dst, src in ((id_sb.ap()[:, :], identity), (tri_sb.ap()[:, :], trimask),
                         (qb8.ap()[:, :], qbdup), (kb8.ap()[:, :], kbdup)):
            P.dma("sp", cC, dst, src)
        P.dma("sp", cXR, xres, x_in)
        pstage = vf(A3, 36896, 256)
        P.dma("sp", cC, pstage, proj4)
        P.wait("dve", cC)
        P.op("dve", lambda e: e.tensor_copy(projT_aug.ap()[:, :], pstage), sDVE)
        P.op("dve", lambda e: e.tensor_scalar_mul(qb8.ap()[:, :], qb8.ap()[:, :], 0.125), sDVE)
        P.op("dve", lambda e: e.tensor_scalar_mul(kb8.ap()[:, :], kb8.ap()[:, :], 0.125), sDVE)
        dve_p0 = sDVE.n

        wq_stage = [v1(A1, cix * 1024, 1024) for cix in range(16)]
        for cix in range(16):
            P.dma("gp", gW[cix % 4], wq_stage[cix], Wq[cix * 128:(cix + 1) * 128, :])
        P.wait_all_dma("pe", gW)
        P.wait("pe", cC)
        for r in range(8):
            for c in range(16):
                transpose_blk(wq_stage[c][:, r * 128:(r + 1) * 128],
                              WqT[r][:, c * 128:(c + 1) * 128], (r * 16 + c) % 2)
        kvst = v1(A1, 16384, 1024)
        kust = v1(A1, 17408, 1024)
        vust = v1(A1, 18432, 1024)
        P.dma("gp", gW[0], kvst[0:64, :], Wkv)
        P.dma("gp", gW[1], kust.rearrange("p (c w) -> p c w", c=16),
              Wku.rearrange("(c p) w -> c p w", p=128).transpose([1, 0, 2]))
        P.dma("gp", gW[2], vust.rearrange("p (c w) -> p c w", c=16),
              Wvu.rearrange("(c p) w -> c p w", p=128).transpose([1, 0, 2]))
        P.wait_all_dma("pe", gW[:3])
        for c in range(8):
            transpose_blk(kvst[0:64, c * 128:(c + 1) * 128], wkvT[c], c % 2, width=128, p_in=64)
        for c in range(16):
            transpose_blk(kust[:, c * 64:(c + 1) * 64], kuT[c], c % 2, width=64, p_in=128)
        for c in range(16):
            transpose_blk(vust[:, c * 64:(c + 1) * 64],
                          A2.ap()[0:64, 18432 + c * 128:18432 + (c + 1) * 128], c % 2,
                          width=64, p_in=128)
        pe_prelude = sPE.n

        for e_i in range(N_EXP):
            P.dma("gp", gW[e_i % 4], W1c[e_i * 1024:(e_i + 1) * 1024, :],
                  W1p[e_i, :, :])
            P.dma("gp", gW[e_i % 4], W2c[e_i * 1024:(e_i + 1) * 1024, :],
                  W2[e_i, :, :])
        gw_cast = [g.n for g in gW]

        xt_s, ce_s, xb_s = ln_scratch(16384)
        cent_cur[0], cent_cur[1] = ce_s[0], ce_s[1]
        dve_lnmark = {}
        for i in range(NT):
            sl = i % 2
            if i >= 2:
                P.wait("sp", sDVE, dve_lnmark[i - 2])
            if i == 0:
                P.wait("sp", sDVE, dve_p0)
            P.dma("sp", cX[sl], xt_s[sl], x_in[i * 128:(i + 1) * 128, :])
            P.wait("dve", cX[sl])
            ln_tile(xt_s[sl], xb_s[sl], out_bf=xb_s[sl], slot=sl, ld=(cX[sl],))
            dve_lnmark[i] = sDVE.n
            P.wait("pe", sDVE, dve_lnmark[i])
            for c in range(8):
                transpose_blk(xb_s[sl][:, c * 128:(c + 1) * 128],
                              XT[c][:, i * 128:(i + 1) * 128], c % 2)
        act_p1 = sACT.n

        P.wait("pe", sACT, act_p1)
        lat_mark = {}
        for scnk in range(8):
            bank = pb[scnk % 2]
            if scnk >= 2:
                P.wait("pe", sACT, lat_mark[scnk - 2])
            for kc in range(8):
                P.op("pe", lambda e, b=bank, kc0=kc, s0=scnk, st=(kc == 0), sp=(kc == 7):
                     e.matmul(b.ap()[0:64, 0:512], wkvT[kc0],
                              XT[kc0][:, s0 * 512:(s0 + 1) * 512], start=st, stop=sp),
                     sPE if kc == 7 else None)
            pev = sPE.n
            P.wait("act", sPE, pev)
            P.op("act", lambda e, b=bank, s0=scnk: e.copy(latT[:, s0 * 512:(s0 + 1) * 512],
                                                          b.ap()[0:64, 0:512]), sACT)
            lat_mark[scnk] = sACT.n
        P.wait("pe", sACT)
        P.wait("dve", sDVE)
        P.op("dve", lambda e: e.memset(
            vA.rearrange("p (t w) -> p t w", w=257)[:, :, 256:257], 1.0), sDVE)
        dve_ones = sDVE.n
        P.wait("pe", sDVE, dve_ones)

        for h in range(A_HEADS):
            P.wait("act", sPE)
            P.wait_all_dma("act", cTA)
            qm = {}
            for it, (mc, scnk) in enumerate([(m, sc) for m in range(2) for sc in range(8)]):
                bank = pb[it % 2]
                if it >= 2:
                    P.wait("pe", sACT, qm[it - 2])
                for kc in range(8):
                    P.op("pe", lambda e, b=bank, kc0=kc, m0=mc, s0=scnk, h0=h,
                         st=(kc == 0), sp=(kc == 7):
                         e.matmul(b.ap()[:, 0:512],
                                  WqT[kc0][:, h0 * 256 + m0 * 128:h0 * 256 + (m0 + 1) * 128],
                                  XT[kc0][:, s0 * 512:(s0 + 1) * 512], start=st, stop=sp),
                         sPE if kc == 7 else None)
                pev = sPE.n
                P.wait("act", sPE, pev)
                P.op("act", lambda e, b=bank, m0=mc, s0=scnk:
                     e.copy(qT[m0][:, s0 * 512:(s0 + 1) * 512], b.ap()[:, 0:512]), sACT)
                qm[it] = sACT.n
            P.wait("pe", sACT)
            km = {}
            for it, (mc, scnk) in enumerate([(m, sc) for m in range(2) for sc in range(8)]):
                bank = pb[it % 2]
                if it >= 2:
                    P.wait("pe", sACT, km[it - 2])
                P.op("pe", lambda e, b=bank, m0=mc, s0=scnk, h0=h:
                     e.matmul(b.ap()[:, 0:512], kuT[h0 * 2 + m0],
                              latT[:, s0 * 512:(s0 + 1) * 512], start=True, stop=True), sPE)
                pev = sPE.n
                P.wait("act", sPE, pev)
                P.op("act", lambda e, b=bank, m0=mc, s0=scnk:
                     e.copy(kT[m0][:, s0 * 512:(s0 + 1) * 512], b.ap()[:, 0:512]), sACT)
                km[it] = sACT.n
            P.wait("pe", sACT)
            vm = {}
            for i in range(NT):
                bank = pb[i % 2]
                if i >= 2:
                    P.wait("pe", sACT, vm[i - 2])
                P.op("pe", lambda e, b=bank, i0=i, h0=h:
                     e.matmul(b.ap()[:, 0:256], latT[:, i0 * 128:(i0 + 1) * 128],
                              vu_h[h0], start=True, stop=True), sPE)
                pev = sPE.n
                P.wait("act", sPE, pev)
                P.op("act", lambda e, b=bank, i0=i:
                     e.copy(vA[:, i0 * 257:i0 * 257 + 256], b.ap()[:, 0:256]), sACT)
                vm[i] = sACT.n
            P.wait("pe", sACT)

            for sbk in range(8):
                def emit_S(kt):
                    sbank = pb[kt % 2]
                    for mc in range(2):
                        P.op("pe", lambda e, b=sbank, m0=mc, k0=kt, s0=sbk,
                             st=(mc == 0), sp=(mc == 1):
                             e.matmul(b.ap()[:, 0:512], kT[m0][:, k0 * 128:(k0 + 1) * 128],
                                      qT[m0][:, s0 * 512:(s0 + 1) * 512], start=st, stop=sp),
                             sPE if mc == 1 else None)
                    pev = sPE.n
                    P.wait("act", sPE, pev)
                    P.op("act", lambda e, b=sbank, k0=kt:
                         e.activation(expS[k0 % 2], b.ap()[:, 0:512], AF.Exp,
                                      scale=0.0625), sACT)
                    return sACT.n
                exp_mark = {}
                exp_mark[0] = emit_S(0)
                for kt in range(NT):
                    if kt + 1 < NT:
                        exp_mark[kt + 1] = emit_S(kt + 1)
                    P.wait("pe", sACT, exp_mark[kt])
                    for qt in range(4):
                        P.op("pe", lambda e, q0=qt, k0=kt:
                             e.matmul(pb[2 + q0].ap()[:, 0:257],
                                      expS[k0 % 2][:, q0 * 128:(q0 + 1) * 128],
                                      vA[:, k0 * 257:(k0 + 1) * 257],
                                      start=(k0 == 0), stop=(k0 == NT - 1)),
                             sPE if qt == 3 else None)
                pe_pv = sPE.n
                for qt in range(4):
                    if qt == 0:
                        P.wait("dve", sPE, pe_pv)
                    P.op("dve", lambda e, q0=qt: e.reciprocal(col["rcp"],
                         pb[2 + q0].ap()[:, 256:257]), sDVE)
                    P.wait("dve", sDVE)
                    P.op("dve", lambda e, q0=qt: e.tensor_scalar(
                        attn_nat, pb[2 + q0].ap()[:, 0:256], col["rcp"], None,
                        OP.mult), sDVE)
                    dve_attn = sDVE.n
                    P.wait("pe", sDVE, dve_attn)
                    for mc in range(2):
                        transpose_blk(attn_nat[:, mc * 128:(mc + 1) * 128],
                                      aoT[mc][:, sbk * 512 + qt * 128:sbk * 512 + (qt + 1) * 128],
                                      mc)
                    P.wait("dve", sPE)
            act_head = sACT.n
            P.wait("sp", sACT, act_head)
            for mc in range(2):
                P.dma("sp", cTA[mc], attnTa[(h * 2 + mc) * 128:(h * 2 + mc + 1) * 128, :], aoT[mc])

        P.wait("gp", sPE)
        wo_stage = [v1(A1, cix * 2048, 2048) for cix in range(8)]
        for cix in range(8):
            P.dma("gp", gW[cix % 4], wo_stage[cix], Wo[cix * 128:(cix + 1) * 128, :])
        P.wait_all_dma("pe", gW)
        P.wait("act", sPE)
        for c in range(16):
            for r in range(8):
                transpose_blk(wo_stage[r][:, c * 128:(c + 1) * 128],
                              woT[c][:, r * 128:(r + 1) * 128], (c * 8 + r) % 2)
        act_woT = sACT.n
        pe_woT = sPE.n
        P.wait("pe", sACT, act_woT)
        P.wait_all_dma("sp", cTA)
        P.wait("sp", cXR)

        pe_t3 = {}
        st_t3 = {}
        dv_t3 = {}
        for i in range(NT):
            sl = i % 2
            if i >= 2:
                P.wait("sp", sPE, pe_t3[i - 2])
                P.wait("sp", cST[sl], st_t3[i - 2])
            P.dma("sp", cSG[sl], stage3[sl].rearrange("p (c w) -> p c w", c=16),
                  attnTa.rearrange("(c p) s -> c p s", p=128)[:, :, i * 128:(i + 1) * 128]
                  .transpose([1, 0, 2]))
            P.dma("sp", cX[sl], xt3[sl], xres[i * 128:(i + 1) * 128, :])
            P.wait("pe", cSG[sl])
            P.wait("pe", cX[sl])
            if i >= 2:
                P.wait("pe", sDVE, dv_t3[i - 2])
            for half in range(2):
                bank = pb[half + 2 * sl]
                for c in range(16):
                    P.op("pe", lambda e, b=bank, c0=c, h0=half, s0=sl,
                         st=(c == 0), sp=(c == 15):
                         e.matmul(b.ap()[:, 0:512], stage3[s0][:, c0 * 128:(c0 + 1) * 128],
                                  woT[c0][:, h0 * 512:(h0 + 1) * 512], start=st, stop=sp),
                         sPE if c == 15 else None)
            pev = sPE.n
            pe_t3[i] = pev
            P.wait("dve", sPE, pev)
            P.wait("dve", cX[sl])
            for half in range(2):
                P.op("dve", lambda e, h0=half, s0=sl: e.tensor_tensor(
                    xt3[s0][:, h0 * 512:(h0 + 1) * 512], pb[h0 + 2 * s0].ap()[:, 0:512],
                    xt3[s0][:, h0 * 512:(h0 + 1) * 512], OP.add), sDVE)
            dvev = sDVE.n
            dv_t3[i] = dvev
            P.wait("sp", sDVE, dvev)
            P.dma("sp", cST[sl], xres[i * 128:(i + 1) * 128, :], xt3[sl])
            st_t3[i] = cST[sl].n

        def write_out_from_xres():
            P.wait_all_dma("sp", SP_SEMS)
            for i in range(NT):
                P.dma("sp", cXR, out[i * 128:(i + 1) * 128, :], xres[i * 128:(i + 1) * 128, :])
            P.wait_all_dma("sp", SP_SEMS)
            P.wait_all_dma("gp", GP_SEMS)

        if DEBUG["phase"] == "attn_a":
            write_out_from_xres()
            final_build(P)
            return nc, es

        def ffn_phase(write_out_tiles, pe_entry, st_marks):
            ds_ctr = [0]
            P.wait("sp", sPE, pe_entry)
            P.dma("sp", cB2, b1c, b1cols)
            P.dma("sp", cB2, wcs, wcol_in)
            xtF, ceF, xbF = ln_scratch(26624)
            cent_cur[0], cent_cur[1] = ceF[0], ceF[1]
            P.wait("dve", sPE, pe_entry)
            P.wait("act", sPE, pe_entry)
            P.wait_all_dma("act", cTA)
            P.wait_all_dma("dve", cTA)
            marks = {}
            for t in range(16):
                sl = t % 2
                if t >= 2:
                    P.wait("sp", sDVE, marks[t - 2])
                P.wait("sp", *st_marks[t])
                P.dma("sp", cX[sl], xtF[sl], xres[t * 128:(t + 1) * 128, :])
                P.wait("dve", cX[sl])
                ln_tile(xtF[sl], xbF[sl], out_bf=xn_ffn[t], slot=sl, ld=(cX[sl],))
                marks[t] = sDVE.n
            dve_lnF = sDVE.n

            P.wait("pe", sDVE, dve_lnF)
            P.wait("pe", sDVE)
            P.wait("act", sDVE)
            P.wait_all_dma("act", cST)
            P.wait_all_dma("act", cOUT)
            P.wait_all_dma("act", cX)
            P.wait_all_dma("act", cSG)
            pe_w = {}
            for wi, (eexp, w0, w1, wsegs) in enumerate(rt["windows"]):
                wl = w1 - w0
                if wi >= 2:
                    P.wait("sp", sPE, pe_w[wi - 2])
                P.dma("sp", cPTW[wi % 2], ptw[wi % 2][:, 0:wl], PTg[:, w0:w1])
                P.wait("pe", cPTW[wi % 2])
                for k in range(8):
                    bank = pb[(wi * 8 + k) % 2]
                    P.wait("pe", sACT)
                    nseg = len(wsegs)
                    for si, (tau, ga, gb) in enumerate(wsegs):
                        P.op("pe", lambda e, b=bank, k0=k, t0=tau, a=ga, bb=gb, o=w0,
                             w2=(wi % 2), st=(si == 0), sp=(si == nseg - 1):
                             e.matmul(b.ap()[:, a - o:bb - o],
                                      xn_ffn[t0][:, k0 * 128:(k0 + 1) * 128],
                                      ptw[w2][:, a - o:bb - o], start=st, stop=sp),
                             sPE if si == nseg - 1 else None)
                    pev = sPE.n
                    P.wait("act", sPE, pev)
                    P.op("act", lambda e, b=bank, k0=k, a=w0, bb=w1:
                         e.copy(XgT[k0][:, a:bb], b.ap()[:, 0:bb - a]), sACT)
                pe_w[wi] = sPE.n
            act_gather = sACT.n
            P.wait("pe", sACT, act_gather)

            jobs = []
            for e_i in range(N_EXP):
                if rt["caps"][e_i] == 0:
                    continue
                for (e2, w0, w1, _segs) in [w for w in rt["windows"] if w[0] == e_i]:
                    for j in range(8):
                        jobs.append((e_i, w0, w1 - w0, j))
            jld = {}
            jpe = {}
            w2pe = {}

            def load_jslice(idx):
                e_i, _w0, _wl, j = jobs[idx]
                sl = idx % 2
                P.wait("sp", sPE, jpe.get(idx - 2, 0))
                P.dma("sp", cW1[sl], W1J[sl].rearrange("p (k w) -> p k w", k=8),
                      W1c.rearrange("(a p) w -> a p w", p=128)[e_i * 8:e_i * 8 + 8, :,
                                                              j * 256:(j + 1) * 256]
                      .transpose([1, 0, 2]))
                jld[idx] = cW1[sl]

            exps = [e for e in range(N_EXP) if rt["caps"][e] > 0]

            def load_w2(eidx):
                sl = eidx % 2
                P.wait("sp", sPE, w2pe.get(eidx - 2, 0))
                e_i2 = exps[eidx]
                P.dma("sp", cW2[sl],
                      A2.ap()[:, 4096 + sl * 8192:4096 + sl * 8192 + 8192]
                      .rearrange("p (k w) -> p k w", k=8),
                      W2c.rearrange("(a p) w -> a p w", p=128)[e_i2 * 8:e_i2 * 8 + 8, :, :]
                      .transpose([1, 0, 2]))

            P.wait("sp", sPE)
            P.wait("sp", sACT)
            P.wait("sp", sDVE)
            for gg, vv in zip(gW, gw_cast):
                P.wait("sp", gg, vv)
            load_jslice(0)
            load_jslice(1)
            load_w2(0)
            guw_state = {}
            idx = 0
            eidx = -1
            prev_e = None
            for e_i in range(N_EXP):
                cap = rt["caps"][e_i]
                if cap == 0:
                    continue
                eidx += 1
                if eidx + 1 < len(exps):
                    load_w2(eidx + 1)
                P.wait("sp", sDVE)
                P.dma("sp", cB2, b2r, b2rep[e_i, :, :])
                P.wait("dve", cB2)
                ew = [w for w in rt["windows"] if w[0] == e_i]
                guw = guw_state
                for (e2, w0, w1, _segs) in ew:
                    wl = w1 - w0
                    ngt = wl // 128
                    for j in range(8):
                        sl = idx % 2
                        P.wait("pe", jld[idx])
                        bg = pb[(j % 2) * 2]
                        bu = pb[(j % 2) * 2 + 1]
                        P.wait("pe", sDVE, guw.get((j % 2), 0))
                        for k in range(8):
                            P.op("pe", lambda e, b=bg, k0=k, s2=sl, a=w0, L=wl,
                                 st=(k == 0), sp=(k == 7):
                                 e.matmul(b.ap()[:, 0:L], W1J[s2][:, k0 * 256:k0 * 256 + 128],
                                          XgT[k0][:, a:a + L], start=st, stop=sp),
                                 sPE if k == 7 else None)
                        for k in range(8):
                            P.op("pe", lambda e, b=bu, k0=k, s2=sl, a=w0, L=wl,
                                 st=(k == 0), sp=(k == 7):
                                 e.matmul(b.ap()[:, 0:L],
                                          W1J[s2][:, k0 * 256 + 128:(k0 + 1) * 256],
                                          XgT[k0][:, a:a + L], start=st, stop=sp),
                                 sPE if k == 7 else None)
                        jpe[idx] = sPE.n
                        if idx + 2 < len(jobs):
                            load_jslice(idx + 2)
                        idx += 1
                        pev = sPE.n
                        bgc = b1c[:, e_i * 16 + j:e_i * 16 + j + 1]
                        buc = b1c[:, e_i * 16 + 8 + j:e_i * 16 + 8 + j + 1]
                        P.wait("dve", sPE, pev)
                        P.wait("dve", sDVE)
                        P.op("dve", lambda e, b=bg, c=bgc, L=wl: e.tensor_scalar(
                            gupA[:, 0:L], b.ap()[:, 0:L], c, 7.0, OP.add, OP.min), sDVE)
                        dve_g = sDVE.n
                        P.op("dve", lambda e, b=bu, c=buc, L=wl: e.tensor_scalar(
                            gupB[:, 0:L], b.ap()[:, 0:L], c, 7.0, OP.add, OP.min), sDVE)
                        guw[j % 2] = sDVE.n
                        P.wait("dve", sDVE)
                        P.op("dve", lambda e, L=wl: e.tensor_scalar_max(
                            gupB[:, 0:L], gupB[:, 0:L], -7.0), sDVE)
                        P.wait("act", sDVE, dve_g)
                        P.op("act", lambda e, L=wl: e.activation(
                            gupC[:, 0:L], gupA[:, 0:L], AF.Sigmoid, scale=1.702), sACT)
                        actv = sACT.n
                        P.wait("dve", sACT, actv)
                        P.op("dve", lambda e, L=wl: e.tensor_tensor(
                            gupC[:, 0:L], gupA[:, 0:L], gupC[:, 0:L], OP.mult), sDVE)
                        P.wait("dve", sDVE)
                        P.op("dve", lambda e, j0=j, L=wl: e.scalar_tensor_tensor(
                            ITT[j0][:, 0:L], gupB[:, 0:L], 1.0, gupC[:, 0:L],
                            OP.add, OP.mult), sDVE)
                    dve_inter = sDVE.n
                    P.wait("pe", sDVE, dve_inter)
                    P.wait("pe", cW2[eidx % 2])
                    dmark = {}
                    for gi, (gsub, half) in enumerate([(g, hh) for g in range(ngt) for hh in range(2)]):
                        gt = (w0 + gsub * 128) // 128
                        if half == 0:
                            dslot = ds_ctr[0] % 4
                            ds_ctr[0] += 1
                            P.wait("dve", cDS[dslot])
                            dslots = {}
                        dslots[gsub] = dslot
                        bank = pb[4 + gi % 2]
                        if gi >= 2:
                            P.wait("pe", sDVE, dmark[gi - 2])
                        for k in range(8):
                            P.op("pe", lambda e, b=bank, k0=k, g0=gsub, h0=half,
                                 w2s=W2B2[eidx % 2], st=(k == 0), sp=(k == 7):
                                 e.matmul(b.ap()[:, 0:512],
                                          ITT[k0][:, g0 * 128:(g0 + 1) * 128],
                                          w2s[k0][:, h0 * 512:(h0 + 1) * 512],
                                          start=st, stop=sp),
                                 sPE if k == 7 else None)
                        pev = sPE.n
                        w2pe[eidx] = pev
                        P.wait("dve", sPE, pev)
                        P.wait("dve", sDVE)
                        wv = wcs[:, gt:gt + 1]
                        P.op("dve", lambda e, b=bank, h0=half: e.tensor_tensor(
                            gupA[:, 0:512], b.ap()[:, 0:512],
                            b2r[:, h0 * 512:(h0 + 1) * 512], OP.add), sDVE)
                        P.wait("dve", sDVE)
                        P.op("dve", lambda e, h0=half, g0=dslot, w=wv: e.tensor_scalar(
                            dstage[g0][:, h0 * 512:(h0 + 1) * 512], gupA[:, 0:512],
                            w, None, OP.mult), sDVE)
                        dmark[gi] = sDVE.n
                        if half == 1:
                            dvev = sDVE.n
                            P.wait("sp", sDVE, dvev)
                            P.dma("sp", cDS[dslot], down_sc[gt * 128:(gt + 1) * 128, :],
                                  dstage[dslot])
            xtS = [vf(A3, 12288 + i * 2048, 1024) for i in range(2)]
            blk_of = {k: i for i, k in enumerate(rt["scat_keys"])}
            P.wait_all_dma("sp", cDS)
            P.wait("sp", sPE)
            P.wait("dve", sPE)
            slot_pe = [0, 0]
            pe_pi = {}
            st_pi = {}
            gl = 0
            for pi in range(16):
                pairs = [k for k in rt["scat_keys"] if k[1] == pi]
                sl = pi % 2
                if pi >= 2:
                    P.wait("sp", sPE, pe_pi[pi - 2])
                    P.wait("sp", cST[sl], st_pi[pi - 2])
                    P.wait("sp", cOUT[sl])
                P.dma("sp", cX[sl], xtS[sl], xres[pi * 128:(pi + 1) * 128, :])
                np_ = len(pairs)
                for idx, (gt, _pi) in enumerate(pairs):
                    b = blk_of[(gt, pi)]
                    s2 = gl % 2
                    gl += 1
                    P.wait("sp", sPE, slot_pe[s2])
                    P.dma("sp", cSB[s2], sblk[s2], scat[b * 128:(b + 1) * 128, :])
                    P.dma("sp", cDST[s2], dstage[s2][:, :], down_sc[gt * 128:(gt + 1) * 128, :])
                    P.wait("pe", cSB[s2])
                    P.wait("pe", cDST[s2])
                    if idx == 0:
                        P.wait("pe", sDVE)
                    for half in range(2):
                        P.op("pe", lambda e, ss=s2, h0=half, st=(idx == 0), sp=(idx == np_ - 1):
                             e.matmul(pb[h0].ap()[:, 0:512], sblk[ss],
                                      dstage[ss][:, h0 * 512:(h0 + 1) * 512],
                                      start=st, stop=sp),
                             sPE if half == 1 else None)
                    slot_pe[s2] = sPE.n
                pev = sPE.n
                pe_pi[pi] = pev
                P.wait("dve", sPE, pev)
                P.wait("dve", cX[sl])
                for half in range(2):
                    P.op("dve", lambda e, h0=half, s0=sl: e.tensor_tensor(
                        xtS[s0][:, h0 * 512:(h0 + 1) * 512], pb[h0].ap()[:, 0:512],
                        xtS[s0][:, h0 * 512:(h0 + 1) * 512], OP.add), sDVE)
                dvev = sDVE.n
                P.wait("sp", sDVE, dvev)
                P.dma("sp", cST[sl], xres[pi * 128:(pi + 1) * 128, :], xtS[sl])
                if write_out_tiles:
                    P.dma("sp", cOUT[sl], out[pi * 128:(pi + 1) * 128, :], xtS[sl])
                st_pi[pi] = cST[sl].n

        marks3 = [(cST[i % 2], st_t3[i]) for i in range(16)]
        ffn_phase(False, pe_woT, marks3)
        if DEBUG["phase"] == "ffn1":
            write_out_from_xres()
            final_build(P)
            return nc, es

        xt_s, ce_s, xb_s = ln_scratch(16384)
        cent_cur[0], cent_cur[1] = ce_s[0], ce_s[1]
        P.wait("dve", sPE)
        P.wait("sp", sPE)
        P.wait_all_dma("sp", cST)
        dve_lnmark = {}
        for i in range(NT):
            sl = i % 2
            if i >= 2:
                P.wait("sp", sDVE, dve_lnmark[i - 2])
            P.dma("sp", cX[sl], xt_s[sl], xres[i * 128:(i + 1) * 128, :])
            P.wait("dve", cX[sl])
            ln_tile(xt_s[sl], xb_s[sl], out_bf=xb_s[sl], slot=sl, ld=(cX[sl],))
            dve_lnmark[i] = sDVE.n
            P.wait("pe", sDVE, dve_lnmark[i])
            for c in range(8):
                transpose_blk(xb_s[sl][:, c * 128:(c + 1) * 128],
                              XT[c][:, i * 128:(i + 1) * 128], c % 2)
        act_p5a = sACT.n

        P.wait("gp", sPE)
        for c in range(8):
            P.dma("gp", gW[(2 * c) % 4], qwB[c], qw[c * 128:(c + 1) * 128, :])
            P.dma("gp", gW[(2 * c + 1) % 4], kwB[c], kw[c * 128:(c + 1) * 128, :])
        P.dma("sp", cB2, vbr_b, vb_rep)
        P.wait("pe", sACT, act_p5a)
        P.wait_all_dma("pe", gW)
        P.wait("dve", sDVE)
        P.op("dve", lambda e: e.memset(
            vB.rearrange("p (t w) -> p t w", w=65)[:, :, 64:65], 1.0), sDVE)
        dve_onesb = sDVE.n
        P.wait("pe", sDVE, dve_onesb)

        for h in range(B_HEADS):
            P.wait("act", sPE)
            P.wait("act", cTB)
            P.wait("gp", sPE)
            for c in range(8):
                P.dma("gp", gV[h % 2], vwbuf[h % 2][:, c * 64:(c + 1) * 64],
                      vw[c * 128:(c + 1) * 128, h * 64:(h + 1) * 64])
            P.wait_all_dma("dve", gW)
            P.wait("dve", sPE)
            for j in range(2):
                P.op("dve", lambda e, j0=j, h0=h: e.tensor_copy(
                    qdup.rearrange("p (k j w) -> p k j w", k=8, j=2)[:, :, j0, :],
                    A2.ap()[:, 0:8192].rearrange("p (k w) -> p k w", k=8)[:, :, h0 * 64:(h0 + 1) * 64]), sDVE)
                P.op("dve", lambda e, j0=j, h0=h: e.tensor_copy(
                    kdup.rearrange("p (k j w) -> p k j w", k=8, j=2)[:, :, j0, :],
                    A2.ap()[:, 8192:16384].rearrange("p (k w) -> p k w", k=8)[:, :, h0 * 64:(h0 + 1) * 64]), sDVE)
            dve_dup = sDVE.n
            P.wait("pe", sDVE, dve_dup)
            hq = qb8.ap()[:, h:h + 1]
            hk = kb8.ap()[:, h:h + 1]
            qkm = {}
            for it, (dup, dstT, bia, sc) in enumerate(
                    [(dd, dt2, bb, s2) for (dd, dt2, bb) in
                     ((qdup, QTa, hq), (kdup, KTa, hk)) for s2 in range(8)]):
                bank = pb[it % 2]
                if it >= 2:
                    P.wait("pe", sDVE, qkm[it - 2])
                for k in range(8):
                    P.op("pe", lambda e, b=bank, k0=k, s0=sc, dp=dup,
                         st=(k == 0), sp=(k == 7):
                         e.matmul(b.ap()[:, 0:512], dp[:, k0 * 128:(k0 + 1) * 128],
                                  XT[k0][:, s0 * 512:(s0 + 1) * 512], start=st, stop=sp),
                         sPE if k == 7 else None)
                pev = sPE.n
                P.wait("act", sPE, pev)
                P.op("act", lambda e, b=bank, s0=sc, d=dstT, bi=bia: e.activation(
                    d[64:128, s0 * 512:(s0 + 1) * 512], b.ap()[64:128, 0:512], AF.Square,
                    bias=bi[64:128, :], scale=0.125), sACT)
                actv2 = sACT.n
                P.wait("dve", sACT, actv2)
                P.op("dve", lambda e, b=bank, s0=sc, d=dstT, bi=bia: e.tensor_scalar(
                    d[0:64, s0 * 512:(s0 + 1) * 512], b.ap()[0:64, 0:512], 0.125,
                    bi[0:64, :], OP.mult, OP.add), sDVE)
                qkm[it] = sDVE.n
            P.wait("pe", sDVE)
            act_qk = sACT.n
            fm = {}
            fjobs = ([("qp", mc, sc) for mc in range(2) for sc in range(8)] +
                     [("kp", mc, sc) for mc in range(2) for sc in range(8)] +
                     [("kn", 0, t) for t in range(NT)])
            for it, (kind, mc, sc) in enumerate(fjobs):
                bank = pb[it % 2]
                if it >= 2:
                    P.wait("pe", sACT, fm[it - 2])
                if kind == "qp":
                    P.op("pe", lambda e, b=bank, m0=mc, s0=sc:
                         e.matmul(b.ap()[:, 0:512], projT_aug.ap()[:, m0 * 128:(m0 + 1) * 128],
                                  QTa[:, s0 * 512:(s0 + 1) * 512], start=True, stop=True), sPE)
                elif kind == "kp":
                    P.op("pe", lambda e, b=bank, m0=mc, s0=sc:
                         e.matmul(b.ap()[:, 0:512], projT_aug.ap()[:, m0 * 128:(m0 + 1) * 128],
                                  KTa[:, s0 * 512:(s0 + 1) * 512], start=True, stop=True), sPE)
                else:
                    P.op("pe", lambda e, b=bank, t0=sc:
                         e.matmul(b.ap()[:, 0:256], KTa[:, t0 * 128:(t0 + 1) * 128],
                                  projT_aug.ap()[:, :], start=True, stop=True), sPE)
                pev = sPE.n
                P.wait("act", sPE, pev)
                if kind == "qp":
                    P.op("act", lambda e, b=bank, m0=mc, s0=sc: e.activation(
                        qpT[m0][:, s0 * 512:(s0 + 1) * 512], b.ap()[:, 0:512], AF.Exp), sACT)
                elif kind == "kp":
                    P.op("act", lambda e, b=bank, m0=mc, s0=sc: e.activation(
                        kpT[m0][:, s0 * 512:(s0 + 1) * 512], b.ap()[:, 0:512], AF.Exp), sACT)
                else:
                    P.op("act", lambda e, b=bank, t0=sc: e.activation(
                        kpN[:, t0 * 256:(t0 + 1) * 256], b.ap()[:, 0:256], AF.Exp), sACT)
                fm[it] = sACT.n
            P.wait("pe", sACT)
            P.wait("pe", gV[h % 2])
            if h == 0:
                P.wait("dve", cB2)
            vbm = {}
            for t in range(NT):
                bank = pb[t % 2]
                if t >= 2:
                    P.wait("pe", sDVE, vbm[t - 2])
                for k in range(8):
                    P.op("pe", lambda e, b=bank, k0=k, t0=t, hh=(h % 2),
                         st=(k == 0), sp=(k == 7):
                         e.matmul(b.ap()[:, 0:64], XT[k0][:, t0 * 128:(t0 + 1) * 128],
                                  vwbuf[hh][:, k0 * 64:(k0 + 1) * 64], start=st, stop=sp),
                         sPE if k == 7 else None)
                pev = sPE.n
                P.wait("dve", sPE, pev)
                P.op("dve", lambda e, b=bank, t0=t, h0=h: e.tensor_tensor(
                    vB[:, t0 * 65:t0 * 65 + 64], b.ap()[:, 0:64],
                    vbr_b[:, h0 * 64:(h0 + 1) * 64], OP.add), sDVE)
                vbm[t] = sDVE.n
            P.wait("dve", sDVE)
            P.op("dve", lambda e: e.memset(kvf32[0], 0.0), sDVE)
            P.op("dve", lambda e: e.memset(kvf32[1], 0.0), sDVE)
            dve_v = sDVE.n
            if DEBUG.get("dump") and h == 0:
                P.wait("sp", sDVE, dve_v)
                P.wait("sp", sACT)
                P.dma("sp", cC, dbg[0 * 128:1 * 128, :], QTa)
                P.dma("sp", cC, dbg[1 * 128:2 * 128, :], qpT[0])
                P.dma("sp", cC, dbg[2 * 128:3 * 128, :], qpT[1])
                P.dma("sp", cC, dbg[3 * 128:4 * 128, :], kpT[0])
                P.dma("sp", cC, dbg[4 * 128:5 * 128, :], kpN[:, 0:4096])
                P.dma("sp", cC, dbg[5 * 128:6 * 128, 0:2080], vB[:, 0:2080])
                P.wait("pe", cC)
            P.wait("pe", sDVE, dve_v)

            act_kv_ready = sACT.n
            atm_dve = [0, 0]
            num_dve = [0, 0]
            kv_act = 0
            kv_dve = 0
            for cch in range(NT):
                bA = pb[cch % 2]
                bN = pb[2 + cch % 2]
                P.wait("pe", sDVE, atm_dve[cch % 2])
                for mc in range(2):
                    P.op("pe", lambda e, b=bA, m0=mc, c0=cch, st=(mc == 0), sp=(mc == 1):
                         e.matmul(b.ap()[:, 0:128], kpT[m0][:, c0 * 128:(c0 + 1) * 128],
                                  qpT[m0][:, c0 * 128:(c0 + 1) * 128], start=st, stop=sp),
                         sPE if mc == 1 else None)
                peA = sPE.n
                P.wait("dve", sPE, peA)
                P.op("dve", lambda e, b=bA, c0=cch: e.tensor_tensor(
                    atm[c0 % 2], b.ap()[:, 0:128], tri_sb.ap()[:, :], OP.mult), sDVE)
                atm_dve[cch % 2] = sDVE.n
                P.wait("pe", sDVE, atm_dve[cch % 2])
                if cch > 0:
                    P.wait("pe", sACT, kv_act)
                    P.wait("pe", sDVE, num_dve[cch % 2])
                    for mc in range(2):
                        P.op("pe", lambda e, b=bN, m0=mc, c0=cch, kv=kvsb[(cch - 1) % 2]:
                             e.matmul(b.ap()[:, 0:65], qpT[m0][:, c0 * 128:(c0 + 1) * 128],
                                      kv[m0], start=(m0 == 0), stop=False), None)
                    P.op("pe", lambda e, b=bN, c0=cch:
                         e.matmul(b.ap()[:, 0:65], atm[c0 % 2],
                                  vB[:, c0 * 65:(c0 + 1) * 65], start=False, stop=True), sPE)
                else:
                    P.wait("pe", sDVE, num_dve[cch % 2])
                    P.op("pe", lambda e, b=bN, c0=cch:
                         e.matmul(b.ap()[:, 0:65], atm[c0 % 2],
                                  vB[:, c0 * 65:(c0 + 1) * 65], start=True, stop=True), sPE)
                peN = sPE.n
                num_pe = peN
                if cch > 0:
                    P.wait("pe", sDVE, kv_dve)
                for mc in range(2):
                    P.op("pe", lambda e, m0=mc, c0=cch:
                         e.matmul(pb[4 + m0].ap()[:, 0:65],
                                  kpN[:, c0 * 256 + m0 * 128:c0 * 256 + (m0 + 1) * 128],
                                  vB[:, c0 * 65:(c0 + 1) * 65],
                                  start=True, stop=True),
                         sPE if mc == 1 else None)
                peU = sPE.n
                if cch < NT - 1:
                    P.wait("dve", sPE, peU)
                    P.wait("dve", sDVE)
                    for mc in range(2):
                        P.op("dve", lambda e, m0=mc: e.tensor_tensor(
                            kvf32[m0], kvf32[m0], pb[4 + m0].ap()[:, 0:65], OP.add), sDVE)
                    kv_dve = sDVE.n
                    P.wait("act", sDVE, kv_dve)
                    P.wait("act", sPE, num_pe)
                    P.op("act", lambda e, kv=kvsb[cch % 2]: e.copy(kv[0], kvf32[0]), sACT)
                    P.op("act", lambda e, kv=kvsb[cch % 2]: e.copy(kv[1], kvf32[1]), sACT)
                    kv_act = sACT.n
                P.wait("dve", sPE, peN)
                P.op("dve", lambda e, b=bN: e.reciprocal(col["rcp"], b.ap()[:, 64:65]), sDVE)
                P.wait("dve", sDVE)
                P.op("dve", lambda e, b=bN: e.tensor_scalar(
                    battn, b.ap()[:, 0:64], col["rcp"], None, OP.mult), sDVE)
                num_dve[cch % 2] = sDVE.n
                P.wait("pe", sDVE, num_dve[cch % 2])
                transpose_blk(battn, atT[0:64, cch * 128:(cch + 1) * 128],
                              cch % 2, width=64, p_in=128)
                P.wait("dve", sPE)
            act_scan = sACT.n
            P.wait("sp", sACT, act_scan)
            P.dma("sp", cTB, attnTb[h * 64:(h + 1) * 64, :], atT)

        pe_b_end = sPE.n
        P.wait("gp", sPE)
        for c in range(8):
            P.dma("gp", gW[c % 4], owB[c], ow[c * 128:(c + 1) * 128, :])
        P.wait("sp", sDVE)
        P.wait("sp", sPE)
        P.dma("sp", cC, obrf, ob_rep)
        P.dma("sp", cC, gmrf, gam_rep)
        P.dma("sp", cC, btrf, bet_rep)
        xt5 = [vf(A3, 12288 + i * 2048, 1024) for i in range(2)]
        ce5 = [vf(A3, 16384 + i * 2048, 1024) for i in range(2)]
        xb5 = [v1(A3, 20480 + i * 1024, 1024) for i in range(2)]
        cent_cur[0], cent_cur[1] = ce5[0], ce5[1]
        P.wait_all_dma("pe", gW)
        pe_t5 = {}
        st_t5 = {}
        dv_t5 = {}
        for i in range(NT):
            sl = i % 2
            if i >= 2:
                P.wait("sp", sPE, pe_t5[i - 2])
                P.wait("sp", cST[sl], st_t5[i - 2])
                P.wait("sp", cOUT[sl])
            if i < 2:
                P.wait("sp", sACT)
                P.wait("sp", sPE)
                P.wait("sp", cTB)
            P.dma("sp", cSG[sl], bstage[sl].rearrange("p (c w) -> p c w", c=8),
                  attnTb.rearrange("(c p) s -> c p s", p=128)[:, :, i * 128:(i + 1) * 128]
                  .transpose([1, 0, 2]))
            P.dma("sp", cX[sl], xt5[sl], xres[i * 128:(i + 1) * 128, :])
            P.wait("pe", cSG[sl])
            P.wait("pe", cX[sl])
            if i >= 2:
                P.wait("pe", sDVE, dv_t5[i - 2])
            for half in range(2):
                bank = pb[half + 2 * sl]
                for c in range(8):
                    P.op("pe", lambda e, b=bank, c0=c, h0=half, s0=sl,
                         st=(c == 0), sp=(c == 7):
                         e.matmul(b.ap()[:, 0:512], bstage[s0][:, c0 * 128:(c0 + 1) * 128],
                                  owB[c0][:, h0 * 512:(h0 + 1) * 512], start=st, stop=sp),
                         sPE if c == 7 else None)
            pev = sPE.n
            pe_t5[i] = pev
            P.wait("dve", cX[sl])
            if i == 0:
                P.wait("dve", cC)
            ln_tile(xt5[sl], xb5[sl], out_f32=xn2f[sl], slot=sl, eps=LN_EPS,
                    ld=(cX[sl],))
            P.wait("dve", sPE, pev)
            P.wait("dve", sDVE)
            for half in range(2):
                P.op("dve", lambda e, h0=half, s0=sl: e.tensor_tensor(
                    residf[s0][:, h0 * 512:(h0 + 1) * 512], pb[h0 + 2 * s0].ap()[:, 0:512],
                    obrf[:, h0 * 512:(h0 + 1) * 512], OP.add), sDVE)
            P.wait("dve", sDVE)
            dv_t5[i] = sDVE.n
            P.op("dve", lambda e, s0=sl: e.tensor_tensor(residf[s0], residf[s0], xn2f[s0], OP.add), sDVE)
            ln_tile(residf[sl], xb5[sl], out_f32=xn2f[sl], eps=LN2_EPS, slot=sl,
                    gamma_beta=(gmrf, btrf), ld=(sDVE,))
            P.wait("dve", sDVE)
            P.op("dve", lambda e, s0=sl: e.tensor_tensor(xt5[s0], xt5[s0], xn2f[s0], OP.add), sDVE)
            dvev = sDVE.n
            P.wait("sp", sDVE, dvev)
            P.dma("sp", cST[sl], xres[i * 128:(i + 1) * 128, :], xt5[sl])
            P.dma("sp", cOUT[sl], out[i * 128:(i + 1) * 128, :], xt5[sl])
            st_t5[i] = cST[sl].n

        if DEBUG["phase"] == "attn_b":
            P.wait_all_dma("sp", SP_SEMS)
            P.wait_all_dma("gp", GP_SEMS)
            final_build(P)
            return nc, es

        marks5 = [(cST[i % 2], st_t5[i]) for i in range(16)]
        ffn_phase(True, pe_b_end, marks5)
        P.wait_all_dma("sp", SP_SEMS)
        P.wait_all_dma("gp", GP_SEMS)
        final_build(P)
        return nc, es


def final_build(P):
    nc = P.nc
    with nc.Block() as blk:
        @blk.gpsimd
        def _(g):
            for f in P.q["gp"]:
                f(g)

        @blk.sync
        def _(s):
            for f in P.q["sp"]:
                f(s)

        @blk.tensor
        def _(t):
            for f in P.q["pe"]:
                f(t)

        @blk.scalar
        def _(sc):
            for f in P.q["act"]:
                f(sc)

        @blk.vector
        def _(v):
            for f in P.q["dve"]:
                f(v)


def prep_inputs(inputs):
    rt = build_routing(inputs["L0_ffn_router_indices"], inputs["L0_ffn_routing_weights"])
    f32 = np.float32
    cm = {}
    cm["identity"] = np.eye(128, dtype=np.float16)
    cm["trimask"] = np.triu(np.ones((128, 128), dtype=np.float32)).astype(np.float16)
    for nm, key in (("Wq", "L0_attn_a_Wq"), ("Wkv", "L0_attn_a_Wkv_down"),
                    ("Wku", "L0_attn_a_Wk_up"), ("Wvu", "L0_attn_a_Wv_up"),
                    ("Wo", "L0_attn_a_Wo"), ("qw", "L0_attn_b_q_w"),
                    ("kw", "L0_attn_b_k_w"), ("vw", "L0_attn_b_v_w"),
                    ("ow", "L0_attn_b_out_w"), ("W2", "L0_ffn_down_proj")):
        cm[nm] = np.ascontiguousarray(np.asarray(inputs[key], dtype=f32))
    qb = np.asarray(inputs["L0_attn_b_q_b"], dtype=f32)
    kb = np.asarray(inputs["L0_attn_b_k_b"], dtype=f32)
    cm["qbdup"] = np.ascontiguousarray(
        np.stack([np.concatenate([qb[h * 64:(h + 1) * 64]] * 2) for h in range(16)], axis=1))
    cm["kbdup"] = np.ascontiguousarray(
        np.stack([np.concatenate([kb[h * 64:(h + 1) * 64]] * 2) for h in range(16)], axis=1))
    for nm, key in (("vb_rep", "L0_attn_b_v_b"), ("ob_rep", "L0_attn_b_out_b"),
                    ("gam_rep", "L0_attn_b_gamma"), ("bet_rep", "L0_attn_b_beta")):
        cm[nm] = np.ascontiguousarray(np.broadcast_to(np.asarray(inputs[key], dtype=f32), (128, D)))
    proj = np.asarray(inputs["L0_attn_b_proj"], dtype=f32)
    p4 = np.full((128, 256), -0.5, dtype=f32)
    p4[0:64, :] = proj.T
    cm["proj4"] = np.ascontiguousarray(p4)
    W1 = np.asarray(inputs["L0_ffn_gate_up_proj"], dtype=f32)
    perm = np.concatenate([np.arange(0, 2048, 2), np.arange(1, 2048, 2)])
    permj = np.concatenate([np.concatenate([perm[j * 128:(j + 1) * 128],
                                            perm[1024 + j * 128:1024 + (j + 1) * 128]])
                            for j in range(8)])
    cm["W1p"] = np.ascontiguousarray(W1[:, :, permj])
    b1 = np.asarray(inputs["L0_ffn_gate_up_proj_bias"], dtype=f32)[:, perm]
    cm["b1cols"] = np.ascontiguousarray(
        np.concatenate([b1[e].reshape(16, 128).T for e in range(N_EXP)], axis=1))
    cm["b2rep"] = np.ascontiguousarray(
        np.broadcast_to(np.asarray(inputs["L0_ffn_down_proj_bias"], dtype=f32)[:, None, :],
                        (N_EXP, 128, D)))
    cm["PTg"] = rt["PT"]
    cm["scat"] = np.ascontiguousarray(rt["scat_blocks"].reshape(-1, 128))
    return rt, cm


def kernel(**inputs):
    rt, cm = prep_inputs(inputs)
    nc, es = build_program(rt)
    x = np.ascontiguousarray(np.asarray(inputs["L0_attn_a_x"], dtype=np.float32))
    in_maps = []
    for c in range(NCORES):
        m = dict(cm)
        m["x"] = np.ascontiguousarray(x[c])
        m["wcol"] = rt["wcol"] if c == 0 else np.zeros_like(rt["wcol"])
        in_maps.append(m)
    res = run_bass_kernel_spmd(nc, in_maps, core_ids=list(range(NCORES)))
    return np.stack([res.results[c]["out"] for c in range(NCORES)], axis=0)
